# version 29
# baseline (speedup 1.0000x reference)
import math
from contextlib import ExitStack

import numpy as np
import ml_dtypes

import concourse.bass as bass
import concourse.mybir as mybir
from concourse.bass_utils import run_bass_kernel_spmd

F32 = mybir.dt.float32
BF16 = mybir.dt.bfloat16
ALU = mybir.AluOpType
ACTF = mybir.ActivationFunctionType
AX = mybir.AxisListType
BF = ml_dtypes.bfloat16

D = 1024
HID = 2048
NE = 8
NCORES = 8
BIG = 1e30



def _chunkmaj(a, p=128):
    R, C = a.shape
    return np.ascontiguousarray(
        a.reshape(R // p, p, C).transpose(1, 0, 2).reshape(p, (R // p) * C)
    )


def _make_consts():
    N = D
    j = np.arange(N, dtype=np.float64)
    f = np.arange(512, dtype=np.float64)
    ang = 2.0 * np.pi * np.outer(j, f) / N
    cosm = np.cos(ang)
    sinm = np.sin(ang)
    alt = np.where(np.arange(N) % 2 == 0, 1.0, -1.0)

    ft = np.zeros((N, N), dtype=np.float64)
    ft[:, 0:512] = cosm
    ft[:, 512] = alt
    ft[:, 513:1024] = -sinm[:, 1:512]
    s = np.ones((N,), dtype=np.float64)
    s[1:512] = np.sqrt(2.0)
    s[513:1024] = np.sqrt(2.0)
    ft_k = ft * s[None, :]
    ft_vq = ft
    ft_q = ft.copy()
    ft_q[:, 513:1024] = -ft_q[:, 513:1024]

    n_ = np.arange(N, dtype=np.float64)
    angn = 2.0 * np.pi * np.outer(f, n_) / N
    g = np.zeros((N, N), dtype=np.float64)
    c = 1.0 / (N * math.sqrt(D))
    g[0, :] = c
    g[1:512, :] = math.sqrt(2.0) * np.cos(angn[1:512, :]) * c
    g[512, :] = alt * c
    g[513:1024, :] = -math.sqrt(2.0) * np.sin(angn[1:512, :]) * c

    return {
        "ftk": _chunkmaj(ft_k.astype(np.float32)),
        "ftvq": _chunkmaj(ft_vq.astype(np.float32)),
        "ftq": _chunkmaj(ft_q.astype(np.float32)),
        "gmat": _chunkmaj(g.astype(np.float32)),
        "ident": np.eye(128, dtype=np.float32),
        "ltri": np.triu(np.ones((128, 128), dtype=np.float32)),
        "onesrow": np.ones((1, 128), dtype=np.float32),
        "onescol": np.ones((128, 1), dtype=np.float32),
    }


def _pieces_16(w):
    a = w.reshape(16, 128, 8, 128)
    return np.ascontiguousarray(a.transpose(0, 3, 2, 1).reshape(16, 128, 1024))




def build_nc(T=4096, TP=4096, CH=1024):
    assert T % CH == 0 and CH % 128 == 0 and TP % 128 == 0
    NT = T // 128
    NPT = TP // 128
    NCH = T // CH
    TPC = CH // 128

    nc = bass.Bass("TRN2", target_bir_lowering=False)

    x_d = nc.dram_tensor("x", [T, D], F32, kind="ExternalInput")
    xp_d = nc.dram_tensor("xp", [TP, D], F32, kind="ExternalInput")
    mask_d = nc.dram_tensor("mask", [1, 1], F32, kind="ExternalInput")
    wk_d = nc.dram_tensor("Wk", [128, 8 * D], F32, kind="ExternalInput")
    wv_d = nc.dram_tensor("Wv", [128, 8 * D], F32, kind="ExternalInput")
    wq_d = nc.dram_tensor("Wq", [128, 8 * D], F32, kind="ExternalInput")
    bk_d = nc.dram_tensor("bk", [128, 8], F32, kind="ExternalInput")
    bv_d = nc.dram_tensor("bv", [128, 8], F32, kind="ExternalInput")
    bq_d = nc.dram_tensor("bq", [128, 8], F32, kind="ExternalInput")
    ftk_d = nc.dram_tensor("ftk", [128, 8 * D], F32, kind="ExternalInput")
    ftvq_d = nc.dram_tensor("ftvq", [128, 8 * D], F32, kind="ExternalInput")
    ftq_d = nc.dram_tensor("ftq", [128, 8 * D], F32, kind="ExternalInput")
    gm_d = nc.dram_tensor("gmat", [128, 8 * D], F32, kind="ExternalInput")
    id_d = nc.dram_tensor("ident", [128, 128], F32, kind="ExternalInput")
    lt_d = nc.dram_tensor("ltri", [128, 128], F32, kind="ExternalInput")
    or_d = nc.dram_tensor("onesrow", [1, 128], F32, kind="ExternalInput")
    oc_d = nc.dram_tensor("onescol", [128, 1], F32, kind="ExternalInput")
    wia_d = nc.dram_tensor("w_inT_a", [16, 128, D], F32, kind="ExternalInput")
    wib_d = nc.dram_tensor("w_inT_b", [16, 128, D], F32, kind="ExternalInput")
    w1a_d = nc.dram_tensor("w1T_a", [16, 128, D], F32, kind="ExternalInput")
    w1b_d = nc.dram_tensor("w1T_b", [16, 128, D], F32, kind="ExternalInput")
    wo_d = nc.dram_tensor("w_outT", [128, 16 * D], F32, kind="ExternalInput")
    w2_d = nc.dram_tensor("w2T", [NE, 128, 16 * D], F32, kind="ExternalInput")
    wg_d = nc.dram_tensor("wgT", [128, 80], F32, kind="ExternalInput")
    gb_d = nc.dram_tensor("gbeb", [1, 10], F32, kind="ExternalInput")
    out_d = nc.dram_tensor("out", [T, D], F32, kind="ExternalOutput")
    x1s_d = nc.dram_tensor("x1s", [T, D], F32)
    wabs_d = nc.dram_tensor("wabs", [4, 16, 128, D], BF16)
    w2s_d = nc.dram_tensor("w2s", [NE + 1, 128, 16 * D], BF16)

    PE, DVE, ACT, SP, PL = nc.tensor, nc.vector, nc.scalar, nc.sync, nc.gpsimd

    with ExitStack() as glob:
        sem = lambda name: glob.enter_context(nc.semaphore(name))
        sb = lambda name, shape, dt: glob.enter_context(nc.sbuf_tensor(name, shape, dt))

        id_b = sb("id_b", [128, 128], BF16)
        lt_b = sb("lt_b", [128, 128], BF16)
        orow_b = sb("orow_b", [1, 128], BF16)
        ocol_b = sb("ocol_b", [128, 1], BF16)
        mask_f = sb("mask_f", [1, 1], F32)
        eps_c = sb("eps_c", [128, 1], F32)
        cry_b = sb("cry_b", [1, 2 * D], BF16)

        s_cst = sem("s_cst")
        s_cvt = sem("s_cvt")
        PL.dma_start(id_b[:, :], id_d[:, :]).then_inc(s_cst, 16)
        PL.dma_start(lt_b[:, :], lt_d[:, :]).then_inc(s_cst, 16)
        PL.dma_start(orow_b[:, :], or_d[:, :]).then_inc(s_cst, 16)
        PL.dma_start(ocol_b[:, :], oc_d[:, :]).then_inc(s_cst, 16)
        PL.dma_start(mask_f[:, :], mask_d[:, :]).then_inc(s_cst, 16)
        CST = 80
        s_ini = sem("s_ini")
        DVE.memset(eps_c[:, :], 1e-5)
        DVE.drain()
        DVE.engine_nop().then_inc(s_ini, 1)
        ACT.wait_ge(s_ini, 1)

        s_ft = sem("s_ft")
        s_wr = [sem(f"s_wr{i}") for i in range(2)]
        s_bc = sem("s_bc")
        sx = [sem(f"sx{i}") for i in range(6)]
        ss = [sem(f"ss{i}") for i in range(2)]
        st_f = sem("st_f")
        sc_f = sem("sc_f")
        sv_td = sem("sv_td")
        sv_st = sem("sv_st")
        sc_ln = sem("sc_ln")
        sv_ln = sem("sv_ln")
        st_xT = sem("st_xT")
        sc_xT = sem("sc_xT")
        st_k = sem("st_k")
        st_v = sem("st_v")
        st_q = sem("st_q")
        sc_k = sem("sc_k")
        sc_v = sem("sc_v")
        sc_cq = sem("sc_cq")
        st_cy = sem("st_cy")
        s_cy2 = [sem(f"s_cy2{i}") for i in range(2)]
        sv_kv = sem("sv_kv")
        st_cs = sem("st_cs")
        sv_cr = sem("sv_cr")
        sv_cy = sem("sv_cy")
        st_m = sem("st_m")
        sc_m = sem("sc_m")
        sv_m = sem("sv_m")
        st_cm = sem("st_cm")
        sv_q1 = sem("sv_q1")
        sc_nm = sem("sc_nm")
        sv_qn = sem("sv_qn")
        st_qT = sem("st_qT")
        sc_qT = sem("sc_qT")
        st_id = sem("st_id")
        sv_x1 = sem("sv_x1")

        with ExitStack() as stA:
            sbA = lambda name, shape, dt: stA.enter_context(nc.sbuf_tensor(name, shape, dt))
            wtk_b = sbA("wtk_b", [128, 8 * D], BF16)
            wtv_b = sbA("wtv_b", [128, 8 * D], BF16)
            wtq_b = sbA("wtq_b", [128, 8 * D], BF16)
            biasP = sbA("biasP", [1, 3 * D], BF16)

            ps_tr = stA.enter_context(nc.psum_tensor("ps_tr", [128, D], F32))
            ps_kv = stA.enter_context(nc.psum_tensor("ps_kv", [128, D], F32))
            ps_q = stA.enter_context(nc.psum_tensor("ps_q", [128, D], F32))

            xf = sbA("xf", [128, 4 * D], F32)
            xln_b = sbA("xln_b", [128, 2 * D], BF16)
            xT_b = sbA("xT_b", [128, 2 * D], BF16)
            stats = sbA("stats", [128, 12], F32)
            aggr = sbA("aggr", [128, 4], F32)
            rcol = sbA("rcol", [128, 4], F32)

            with ExitStack() as stF:
                sbF = lambda name, shape, dt: stF.enter_context(nc.sbuf_tensor(name, shape, dt))
                ps_cy = stF.enter_context(nc.psum_tensor("ps_cy", [1, D], F32))
                ftb = sbF("ftb", [128, 8 * D], BF16)
                wraw = sbF("wraw", [128, 8 * D], BF16)
                bcol = sbF("bcol", [128, 24], BF16)
                xln_all = sbF("xln_all", [128, NPT * D], BF16)
                m_sb = sbF("m_sb", [128, 8 * D], BF16)
                hh = xT_b
                xbr_row = sbF("xbr_row", [1, D], BF16)
                xbr_col = sbF("xbr_col", [128, 8], BF16)
                axcx = sbF("axcx", [1, 2 * D], BF16)
                s_sb = sbF("s_sb", [1, 4 * 512], BF16)
                tr1 = sbF("tr1", [1, 2 * 512], BF16)
                ps_cs = ps_cy

                PL.dma_start(bcol[:, 0:8], bk_d[:, :]).then_inc(s_bc, 16)
                PL.dma_start(bcol[:, 8:16], bv_d[:, :]).then_inc(s_bc, 16)
                PL.dma_start(bcol[:, 16:24], bq_d[:, :]).then_inc(s_bc, 16)

                nfold = 0
                for pi, (w_d, ft_d, wt_b) in enumerate(
                        [(wk_d, ftk_d, wtk_b), (wv_d, ftvq_d, wtv_b),
                         (wq_d, ftq_d, wtq_b)]):
                    if pi >= 1:
                        PL.wait_ge(st_f, 9 * pi)
                    PL.dma_start(ftb[:, :], ft_d[:, :]).then_inc(s_ft, 16)
                    PL.dma_start(wraw[:, :], w_d[:, :]).then_inc(s_ft, 16)
                    PE.wait_ge(s_ft, 32 * (pi + 1))
                    if pi == 0:
                        PE.wait_ge(s_cst, CST)
                        PE.wait_ge(s_bc, 48)
                    wbuf = wraw
                    for dc in range(8):
                        if nfold >= 2:
                            PE.wait_ge(sc_f, nfold - 1)
                        ps = ps_tr if (nfold % 2 == 0) else ps_kv
                        for jc in range(8):
                            for h in range(2):
                                mi = PE.matmul(
                                    ps[:, 512 * h:512 * (h + 1)],
                                    wbuf[:, jc * D + dc * 128: jc * D + (dc + 1) * 128],
                                    ftb[:, jc * D + 512 * h: jc * D + 512 * (h + 1)],
                                    start=(jc == 0), stop=(jc == 7))
                        mi.then_inc(st_f, 1)
                        nfold += 1
                        ACT.wait_ge(st_f, nfold)
                        ACT.copy(wt_b[:, dc * D:(dc + 1) * D], ps[:, :]).then_inc(sc_f, 1)
                    if pi > 0:
                        PE.wait_ge(sc_f, nfold)
                    for jc in range(8):
                        for h in range(2):
                            mi = PE.matmul(
                                ps_cs[:, 512 * h:512 * (h + 1)],
                                bcol[:, 8 * pi + jc: 8 * pi + jc + 1],
                                ftb[:, jc * D + 512 * h: jc * D + 512 * (h + 1)],
                                start=(jc == 0), stop=(jc == 7))
                    mi.then_inc(st_f, 1)
                    nfold += 1
                    ACT.wait_ge(st_f, nfold)
                    ACT.copy(biasP[0:1, pi * D:(pi + 1) * D], ps_cs[0:1, :]).then_inc(sc_f, 1)
                FOLD = nfold
                CVT = 16 * 13

                x_t = x_d.ap().rearrange("(n p) d -> n p d", p=128)
                xp_t = xp_d.ap().rearrange("(n p) d -> n p d", p=128)
                x1s_t = x1s_d.ap().rearrange("(n p) d -> n p d", p=128)
                GTOT = NPT + NT

                def xsrc(g):
                    return xf[:, (g % 4) * D:(g % 4 + 1) * D]

                def sp_load(g):
                    if g >= 4:
                        SP.wait_ge(sv_td, g - 3)
                    ap = xp_t[g] if g < NPT else x_t[g - NPT]
                    SP.dma_start(xsrc(g), ap).then_inc(sx[g % 4], 16)

                def d_stats(g):
                    DVE.wait_ge(sx[g % 4], 16 * (g // 4 + 1))
                    DVE.bn_stats(stats[:, 0:6], xsrc(g)[:, 0:512])
                    DVE.bn_stats(stats[:, 6:12], xsrc(g)[:, 512:1024])
                    DVE.drain()
                    DVE.bn_aggr(aggr[:, 2 * (g % 2):2 * (g % 2) + 2], stats[:, :])
                    DVE.drain()
                    DVE.engine_nop().then_inc(sv_st, 1)

                def a_lnexp(g):
                    r = g % 2
                    ACT.wait_ge(sv_st, g + 1)
                    if g >= 2:
                        ACT.wait_ge(sv_ln, g - 1)
                    ACT.activation(rcol[:, 2 * r + 1:2 * r + 2], aggr[:, 2 * r + 1:2 * r + 2],
                                   ACTF.Ln, bias=eps_c[:, 0:1])
                    ACT.drain()
                    ACT.activation(rcol[:, 2 * r:2 * r + 1], rcol[:, 2 * r + 1:2 * r + 2],
                                   ACTF.Exp, scale=-0.5)
                    ACT.drain()
                    ACT.nop().then_inc(sc_ln, 1)

                def d_apply(g):
                    r = g % 2
                    DVE.wait_ge(sc_ln, g + 1)
                    if g < NPT:
                        dst = xln_all[:, g * D:(g + 1) * D]
                    else:
                        t = g - NPT
                        if t >= 2:
                            DVE.wait_ge(st_xT, t - 1)
                        dst = xln_b[:, r * D:(r + 1) * D]
                    DVE.tensor_scalar(dst, xsrc(g),
                                      aggr[:, 2 * r:2 * r + 1], rcol[:, 2 * r:2 * r + 1],
                                      ALU.subtract, ALU.mult)
                    DVE.drain()
                    DVE.engine_nop().then_inc(sv_ln, 1)
                    if g < NPT:
                        DVE.engine_nop().then_inc(sv_td, 1)

                def ln_chain(g):
                    d_stats(g)
                    a_lnexp(g)
                    d_apply(g)

                def p_xbar(pt):
                    PE.wait_ge(sv_ln, pt + 1)
                    for h in range(2):
                        mi = PE.matmul(ps_cy[0:1, 512 * h:512 * (h + 1)], ocol_b[:, :],
                                       xln_all[:, pt * D + 512 * h: pt * D + 512 * (h + 1)],
                                       start=(pt == 0), stop=(pt == NPT - 1))
                    mi.then_inc(st_cy, 1)

                def p_trX(t):
                    PE.wait_ge(sv_ln, NPT + t + 1)
                    if t == 0:
                        PE.wait_ge(sv_m, MV_TOT[0])
                    else:
                        PE.wait_ge(sc_k, t)
                    xl = xln_b[:, (t % 2) * D:(t % 2 + 1) * D]
                    for cc in range(8):
                        mi = PE.matmul(ps_kv[:, cc * 128:(cc + 1) * 128],
                                       xl[:, cc * 128:(cc + 1) * 128], id_b[:, :],
                                       start=True, stop=True)
                    mi.then_inc(st_xT, 1)

                def a_evacX(t):
                    ACT.wait_ge(st_xT, t + 1)
                    if t >= 2:
                        ACT.wait_ge(st_v, t - 1)
                    ACT.copy(xT_b[:, (t % 2) * D:(t % 2 + 1) * D], ps_kv[:, :]).then_inc(sc_xT, 1)

                def _pj(ps, xt, wt, dc, h):
                    return PE.matmul(ps[:, 512 * h:512 * (h + 1)],
                                     xt[:, dc * 128:(dc + 1) * 128],
                                     wt[:, dc * D + 512 * h: dc * D + 512 * (h + 1)],
                                     start=(dc == 0), stop=False)

                def _pjbias(ps, bi, stop=True):
                    for h in range(2):
                        mi = PE.matmul(ps[:, 512 * h:512 * (h + 1)], orow_b[0:1, :],
                                       biasP[0:1, bi * D + 512 * h: bi * D + 512 * (h + 1)],
                                       start=False, stop=stop)
                    return mi

                def p_projK(t):
                    PE.wait_ge(sc_xT, t + 1)
                    xt = xT_b[:, (t % 2) * D:(t % 2 + 1) * D]
                    for dc in range(8):
                        for h in range(2):
                            _pj(ps_kv, xt, wtk_b, dc, h)
                    _pjbias(ps_kv, 0).then_inc(st_k, 1)

                def p_projQ(t):
                    if t == 0:
                        PE.wait_ge(sc_m, SCM_TOT[0])
                    else:
                        PE.wait_ge(sc_cq, t)
                    xt = xT_b[:, (t % 2) * D:(t % 2 + 1) * D]
                    for dc in range(8):
                        for h in range(2):
                            _pj(ps_q, xt, wtq_b, dc, h)
                    _pjbias(ps_q, 2).then_inc(st_q, 1)

                def a_evacK(t):
                    ACT.wait_ge(st_k, t + 1)
                    if t >= 1:
                        ACT.wait_ge(sv_kv, t)
                    ACT.copy(kvs_b[:, 0:D], ps_kv[:, :]).then_inc(sc_k, 1)

                def p_projV(t):
                    if t == 1:
                        PE.wait_ge(sc_v, 1)
                    elif t >= 2:
                        PE.wait_ge(sv_x1, t - 1)
                    xt = xT_b[:, (t % 2) * D:(t % 2 + 1) * D]
                    for dc in range(8):
                        for h in range(2):
                            _pj(ps_id, xt, wtv_b, dc, h)
                    _pjbias(ps_id, 1).then_inc(st_v, 1)

                def a_evacV(t):
                    ACT.wait_ge(st_v, t + 1)
                    ACT.copy(kvs_b[:, D:2 * D], ps_id[:, :]).then_inc(sc_v, 1)

                def cmul(dst_ap, pa, pb):
                    t1, t2 = prod[:, 0:512], prod[:, 512:1024]
                    t3, t4 = prod[:, 1024:1536], prod[:, 1536:2048]
                    DVE.tensor_mul(t1, pa[:, 0:512], pb[:, 0:512])
                    DVE.tensor_mul(t2, pa[:, 512:1024], pb[:, 512:1024])
                    DVE.tensor_mul(t3, pa[:, 0:512], pb[:, 512:1024])
                    DVE.tensor_mul(t4, pa[:, 512:1024], pb[:, 0:512])
                    DVE.drain()
                    DVE.tensor_sub(dst_ap[:, 1:512], t1[:, 1:512], t2[:, 1:512])
                    DVE.tensor_copy(dst_ap[:, 0:1], t1[:, 0:1])
                    DVE.tensor_add(dst_ap[:, 513:1024], t3[:, 1:512], t4[:, 1:512])
                    DVE.tensor_copy(dst_ap[:, 512:513], t2[:, 0:1])
                    DVE.drain()

                def d_products(t):
                    DVE.wait_ge(sc_v, t + 1)
                    if t >= 2:
                        DVE.wait_ge(st_cm, t - 1)
                    cmul(kv_b[:, (t % 2) * D:(t % 2 + 1) * D],
                         kvs_b[:, 0:D], kvs_b[:, D:2 * D])
                    DVE.engine_nop().then_inc(sv_kv, 1)

                def p_cum(t):
                    PE.wait_ge(sv_kv, t + 1)
                    if t == 0:
                        PE.wait_ge(sv_cr, 1)
                    else:
                        PE.wait_ge(s_cy2[t % 2], 16 * ((t + 1) // 2))
                        PE.wait_ge(sc_qT, t)
                    cry = cry_b[0:1, (t % 2) * D:(t % 2 + 1) * D]
                    kv = kv_b[:, (t % 2) * D:(t % 2 + 1) * D]
                    for h in range(2):
                        PE.matmul(ps_tr[:, 512 * h:512 * (h + 1)], orow_b[0:1, :],
                                  cry[0:1, 512 * h:512 * (h + 1)], start=True, stop=False)
                        mi = PE.matmul(ps_tr[:, 512 * h:512 * (h + 1)], lt_b[:, :],
                                       kv[:, 512 * h:512 * (h + 1)], start=False, stop=True)
                    mi.then_inc(st_cm, 1)

                def a_evacCQ(t):
                    ACT.wait_ge(st_cm, t + 1)
                    ACT.wait_ge(st_q, t + 1)
                    if t >= 1:
                        ACT.wait_ge(sv_q1, t)
                    ACT.copy(cqs_b[:, 0:D], ps_tr[:, :])
                    ACT.drain()
                    if t + 1 < NT:
                        ACT.dma_start(cry_b[0:1, ((t + 1) % 2) * D:((t + 1) % 2 + 1) * D],
                                      cqs_b[127:128, 0:D]).then_inc(s_cy2[(t + 1) % 2], 16)
                    ACT.copy(cqs_b[:, D:2 * D], ps_q[:, :]).then_inc(sc_cq, 1)

                def d_queried(t):
                    DVE.wait_ge(sc_cq, t + 1)
                    if t >= 1:
                        DVE.wait_ge(sc_nm, t)
                    cmul(qd_b, cqs_b[:, 0:D], cqs_b[:, D:2 * D])
                    DVE.engine_nop().then_inc(sv_q1, 1)

                def a_norm(t):
                    ACT.wait_ge(sv_q1, t + 1)
                    ACT.activation(sq_f[:, :], qd_b[:, :], ACTF.Square, accum_out=nrm[:, 0:1])
                    ACT.drain()
                    ACT.activation(nrm[:, 1:2], nrm[:, 0:1], ACTF.Ln)
                    ACT.drain()
                    ACT.activation(nrm[:, 2:3], nrm[:, 1:2], ACTF.Exp, scale=0.5)
                    ACT.drain()
                    ACT.nop().then_inc(sc_nm, 1)

                def d_qn(t):
                    DVE.wait_ge(sc_nm, t + 1)
                    DVE.tensor_scalar(nrm[:, 3:4], nrm[:, 2:3], 1e-9, None, ALU.add)
                    DVE.drain()
                    DVE.reciprocal(nrm[:, 3:4], nrm[:, 3:4])
                    DVE.drain()
                    if t >= 1:
                        DVE.wait_ge(st_qT, t)
                    DVE.tensor_scalar(qn_b[:, :], qd_b[:, :], nrm[:, 3:4], None, ALU.mult)
                    DVE.drain()
                    DVE.engine_nop().then_inc(sv_qn, 1)

                def p_trQ(t):
                    PE.wait_ge(sv_qn, t + 1)
                    PE.wait_ge(sc_cq, t + 1)
                    for cc in range(8):
                        mi = PE.matmul(ps_tr[:, cc * 128:(cc + 1) * 128],
                                       qn_b[:, cc * 128:(cc + 1) * 128], id_b[:, :],
                                       start=True, stop=True)
                    mi.then_inc(st_qT, 1)

                def a_evacQn(t):
                    ACT.wait_ge(st_qT, t + 1)
                    if t >= 1:
                        ACT.wait_ge(st_id, t)
                    ACT.copy(qnT_b[:, :], ps_tr[:, :]).then_inc(sc_qT, 1)

                def p_idft(t):
                    PE.wait_ge(sc_qT, t + 1)
                    if t == 0:
                        PE.wait_ge(s_ft, 112)
                    if t + 1 < NT:
                        PE.wait_ge(sc_v, t + 2)
                    else:
                        PE.wait_ge(sv_x1, t)
                    for fc in range(8):
                        for h in range(2):
                            mi = PE.matmul(ps_id[:, 512 * h:512 * (h + 1)],
                                           qnT_b[:, fc * 128:(fc + 1) * 128],
                                           gm_b[:, fc * D + 512 * h: fc * D + 512 * (h + 1)],
                                           start=(fc == 0), stop=(fc == 7))
                    mi.then_inc(st_id, 1)

                def d_x1(t):
                    g = NPT + t
                    DVE.wait_ge(st_id, t + 1)
                    if t >= 2:
                        DVE.wait_ge(ss[t % 2], 16 * (t // 2))
                    x1 = x1_sb[:, (t % 2) * D:(t % 2 + 1) * D]
                    DVE.tensor_add(x1, xsrc(g), ps_id[:, :])
                    DVE.drain()
                    DVE.engine_nop().then_inc(sv_x1, 1)
                    DVE.engine_nop().then_inc(sv_td, 1)

                SP.wait_ge(s_cst, CST)
                for g0 in range(min(3, GTOT)):
                    sp_load(g0)
                d_stats(0)
                a_lnexp(0)
                if GTOT > 1:
                    d_stats(1)
                    a_lnexp(1)
                for pt in range(NPT):
                    if pt + 3 < GTOT:
                        sp_load(pt + 3)
                    d_apply(pt)
                    if pt + 2 < GTOT:
                        d_stats(pt + 2)
                        a_lnexp(pt + 2)
                    p_xbar(pt)

                MV_TOT = [0]
                SCM_TOT = [0]
                nm_t, nm_a, nm_v = 0, 0, 0
                PE.wait_ge(sv_ln, NPT)
                for dc in range(8):
                    ps = ps_tr if (dc % 2 == 0) else ps_kv
                    if dc >= 2:
                        PE.wait_ge(sc_m, nm_a - 1)
                    for pt in range(NPT):
                        for h in range(2):
                            mi = PE.matmul(ps[:, 512 * h:512 * (h + 1)],
                                           xln_all[:, pt * D + dc * 128: pt * D + (dc + 1) * 128],
                                           xln_all[:, pt * D + 512 * h: pt * D + 512 * (h + 1)],
                                           start=(pt == 0), stop=(pt == NPT - 1))
                    mi.then_inc(st_m, 1)
                    nm_t += 1
                    ACT.wait_ge(st_m, nm_t)
                    ACT.copy(m_sb[:, dc * D:(dc + 1) * D], ps[:, :]).then_inc(sc_m, 1)
                    nm_a += 1
                ACT.wait_ge(st_cy, NPT)
                ACT.copy(xbr_row[0:1, :], ps_cy[0:1, :]).then_inc(sc_m, 1)
                nm_a += 1
                PE.wait_ge(sc_m, nm_a)
                for cc in range(8):
                    mi = PE.matmul(ps_q[:, cc:cc + 1], xbr_row[0:1, cc * 128:(cc + 1) * 128],
                                   orow_b[0:1, 0:1], start=True, stop=True)
                mi.then_inc(st_m, 1)
                nm_t += 1
                ACT.wait_ge(st_m, nm_t)
                ACT.copy(xbr_col[:, 0:8], ps_q[:, 0:8]).then_inc(sc_m, 1)
                nm_a += 1
                for wi, wt_b in enumerate((wtk_b, wtv_b)):
                    PE.wait_ge(sc_m, nm_a)
                    for cc in range(8):
                        for h in range(2):
                            mi = PE.matmul(ps_cy[0:1, 512 * h:512 * (h + 1)],
                                           xbr_col[:, cc:cc + 1],
                                           wt_b[:, cc * D + 512 * h: cc * D + 512 * (h + 1)],
                                           start=(cc == 0), stop=(cc == 7))
                    mi.then_inc(st_m, 1)
                    nm_t += 1
                    ACT.wait_ge(st_m, nm_t)
                    ACT.copy(axcx[0:1, wi * D:(wi + 1) * D], ps_cy[0:1, :]).then_inc(sc_m, 1)
                    nm_a += 1
                for dc in range(8):
                    psG = ps_tr if (dc % 2 == 0) else ps_kv
                    if dc >= 2:
                        PE.wait_ge(sv_m, nm_v - 1)
                    for c8 in range(8):
                        for h in range(2):
                            mi = PE.matmul(psG[:, 512 * h:512 * (h + 1)],
                                           m_sb[:, c8 * D + dc * 128: c8 * D + (dc + 1) * 128],
                                           wtv_b[:, c8 * D + 512 * h: c8 * D + 512 * (h + 1)],
                                           start=(c8 == 0), stop=(c8 == 7))
                    mi.then_inc(st_m, 1)
                    nm_t += 1
                    DVE.wait_ge(st_m, nm_t)
                    DVE.tensor_mul(hh[:, 0:512], wtk_b[:, dc * D:dc * D + 512], psG[:, 0:512])
                    DVE.tensor_mul(hh[:, 512:1024], wtk_b[:, dc * D + 512:dc * D + 1024],
                                   psG[:, 512:1024])
                    DVE.tensor_mul(hh[:, 1024:1536], wtk_b[:, dc * D:dc * D + 512],
                                   psG[:, 512:1024])
                    DVE.tensor_mul(hh[:, 1536:2048], wtk_b[:, dc * D + 512:dc * D + 1024],
                                   psG[:, 0:512])
                    DVE.drain()
                    DVE.engine_nop().then_inc(sv_m, 1)
                    nm_v += 1
                    PE.wait_ge(sv_m, nm_v)
                    if dc == 0:
                        PE.wait_ge(sc_m, nm_a)
                    PE.matmul(ps_cy[0:1, 0:512], ocol_b[:, :], hh[:, 0:512],
                              start=(dc == 0), stop=(dc == 7))
                    PE.matmul(ps_cy[0:1, 512:1024], ocol_b[:, :], hh[:, 512:1024],
                              start=(dc == 0), stop=(dc == 7))
                    PE.matmul(ps_q[0:1, 0:512], ocol_b[:, :], hh[:, 1024:1536],
                              start=(dc == 0), stop=(dc == 7))
                    mi = PE.matmul(ps_q[0:1, 512:1024], ocol_b[:, :], hh[:, 1536:2048],
                                   start=(dc == 0), stop=(dc == 7))
                    if dc == 7:
                        mi.then_inc(st_m, 1)
                        nm_t += 1
                ACT.wait_ge(st_m, nm_t)
                ACT.copy(s_sb[0:1, 0:1024], ps_cy[0:1, :])
                ACT.drain()
                ACT.copy(s_sb[0:1, 1024:2048], ps_q[0:1, 0:1024]).then_inc(sc_m, 1)
                nm_a += 1
                DVE.wait_ge(sc_m, nm_a)
                for j, (A, B) in enumerate(((0, 0), (1, 1), (0, 1), (1, 0))):
                    bk = biasP[0:1, A * 512:(A + 1) * 512]
                    bv = biasP[0:1, D + B * 512:D + (B + 1) * 512]
                    cx = axcx[0:1, D + B * 512:D + (B + 1) * 512]
                    ax = axcx[0:1, A * 512:(A + 1) * 512]
                    t0 = tr1[0:1, 0:512]
                    DVE.tensor_mul(t0, bk, cx)
                    DVE.drain()
                    DVE.tensor_add(s_sb[0:1, j * 512:(j + 1) * 512],
                                   s_sb[0:1, j * 512:(j + 1) * 512], t0)
                    DVE.drain()
                    DVE.tensor_mul(t0, bv, ax)
                    DVE.drain()
                    DVE.tensor_add(s_sb[0:1, j * 512:(j + 1) * 512],
                                   s_sb[0:1, j * 512:(j + 1) * 512], t0)
                    DVE.drain()
                    DVE.tensor_mul(t0, bk, bv)
                    DVE.drain()
                    DVE.scalar_tensor_tensor(s_sb[0:1, j * 512:(j + 1) * 512], t0,
                                             float(NPT * 128),
                                             s_sb[0:1, j * 512:(j + 1) * 512],
                                             ALU.mult, ALU.add)
                    DVE.drain()
                DVE.tensor_sub(tr1[0:1, 1:512], s_sb[0:1, 1:512], s_sb[0:1, 513:1024])
                DVE.tensor_copy(tr1[0:1, 0:1], s_sb[0:1, 0:1])
                DVE.tensor_copy(tr1[0:1, 512:513], s_sb[0:1, 512:513])
                DVE.tensor_add(tr1[0:1, 513:1024], s_sb[0:1, 1025:1536], s_sb[0:1, 1537:2048])
                DVE.drain()
                DVE.tensor_scalar(cry_b[0:1, 0:D], tr1[0:1, 0:1024],
                                  mask_f[0:1, 0:1], None, ALU.mult)
                DVE.drain()
                DVE.engine_nop().then_inc(sv_cr, 1)
                MV_TOT[0] = nm_v
                SCM_TOT[0] = nm_a
            ps_id = stA.enter_context(nc.psum_tensor("ps_id", [128, D], F32))
            gm_b = sbA("gm_b", [128, 8 * D], BF16)
            PL.wait_ge(sc_f, FOLD)
            PL.dma_start(gm_b[:, :], gm_d[:, :]).then_inc(s_ft, 16)
            PL.wait_ge(sv_cr, 1)
            for wi, wsrc in enumerate((wia_d, wib_d, w1a_d, w1b_d)):
                PL.dma_start(wabs_d.ap()[wi], wsrc.ap()).then_inc(s_cvt, 16)
            PL.dma_start(w2s_d.ap()[0], wo_d.ap()).then_inc(s_cvt, 16)
            for e in range(NE):
                PL.dma_start(w2s_d.ap()[1 + e], w2_d.ap()[e]).then_inc(s_cvt, 16)
            qd_b = sbA("qd_b", [128, D], BF16)
            qn_b = sbA("qn_b", [128, D], BF16)
            qnT_b = sbA("qnT_b", [128, D], BF16)
            x1_sb = sbA("x1_sb", [128, 2 * D], F32)
            nrm = sbA("nrm", [128, 4], F32)
            prod = sbA("prod", [128, 4 * 512], BF16)
            kv_b = sbA("kv_b", [128, 2 * D], BF16)
            kvs_b = sbA("kvs_b", [128, 2 * D], BF16)
            cqs_b = sbA("cqs_b", [128, 2 * D], BF16)
            sq_f = sbA("sq_f", [128, D], F32)

            d_apply(NPT)
            if NT > 1:
                d_apply(NPT + 1)
            p_trX(0)
            a_evacX(0)
            p_projK(0)
            a_evacK(0)
            p_projQ(0)
            p_projV(0)
            a_evacV(0)
            d_products(0)
            for t in range(NT):
                g = NPT + t
                if g + 3 < GTOT:
                    sp_load(g + 3)
                p_cum(t)
                if t + 1 < NT:
                    p_trX(t + 1)
                if t >= 1:
                    p_idft(t - 1)
                if g + 2 < GTOT:
                    ln_chain(g + 2)
                if t >= 1:
                    d_x1(t - 1)
                d_queried(t)
                if t + 1 < NT:
                    a_evacX(t + 1)
                a_evacCQ(t)
                if t + 1 < NT:
                    p_projK(t + 1)
                    a_evacK(t + 1)
                a_norm(t)
                d_qn(t)
                if t + 1 < NT:
                    p_projV(t + 1)
                    a_evacV(t + 1)
                p_trQ(t)
                if t + 1 < NT:
                    p_projQ(t + 1)
                a_evacQn(t)
                if t + 1 < NT:
                    d_products(t + 1)
                if t >= 1:
                    SP.wait_ge(sv_x1, t)
                    SP.dma_start(x1s_t[t - 1, :, :],
                                 x1_sb[:, ((t - 1) % 2) * D:((t - 1) % 2 + 1) * D]
                                 ).then_inc(ss[(t - 1) % 2], 16)
            p_idft(NT - 1)
            d_x1(NT - 1)
            SP.wait_ge(sv_x1, NT)
            SP.dma_start(x1s_t[NT - 1, :, :],
                         x1_sb[:, ((NT - 1) % 2) * D:((NT - 1) % 2 + 1) * D]
                         ).then_inc(ss[(NT - 1) % 2], 16)

        s_ac = sem("s_ac")
        s_out = sem("s_out")
        s_wg = sem("s_wg")
        s_wa = [sem(f"s_wa{i}") for i in range(2)]
        s_wb = [sem(f"s_wb{i}") for i in range(2)]
        s_x1t = [sem(f"s_x1t{i}") for i in range(2)]
        s_w2 = [sem(f"s_w2{i}") for i in range(2)]
        st_tb = sem("st_tb")
        sc_ev = sem("sc_ev")
        st_g = sem("st_g")
        sc_gx = sem("sc_gx")
        sv_g = sem("sv_g")
        sv_se = sem("sv_se")
        sc_gl = sem("sc_gl")
        st_ab = sem("st_ab")
        sc_sil = sem("sc_sil")
        sv_sT = sem("sv_sT")
        st_r = sem("st_r")
        sv_acc = sem("sv_acc")
        sv_fin = sem("sv_fin")

        SS0_TOT = 16 * ((NT + 1) // 2)
        SS1_TOT = 16 * (NT // 2)

        with ExitStack() as stB:
            sbB = lambda name, shape, dt: stB.enter_context(nc.sbuf_tensor(name, shape, dt))
            acc = sbB("acc", [128, TPC * D], F32)
            hiT = sbB("hiT", [128, 8 * CH], BF16)
            hsT = sbB("hsT", [128, 16 * CH], BF16)
            w2_b = [sbB(f"w2_{i}", [128, 16 * D], BF16) for i in range(2)]
            wab_b = [sbB(f"wab_{i}", [128, 2 * D], BF16) for i in range(2)]
            wg_hi = sbB("wg_hi", [128, 80], BF16)
            wg_f = sbB("wg_f", [128, 80], F32)
            gbeb_f = sbB("gbeb_f", [1, 10], F32)
            gbeb_b = sbB("gbeb_b", [1, 20], BF16)
            wse = sbB("wse", [128, NE * TPC], F32)
            hi_b = sbB("hi_b", [128, TPC * D], BF16)
            x1t = sbB("x1t", [128, 2 * D], F32)
            sil_b = sbB("sil_b", [128, 2 * D], BF16)
            st2 = sbB("st2", [128, 12], F32)
            ag2 = sbB("ag2", [128, 2 * TPC], F32)
            rc2 = sbB("rc2", [128, 2 * TPC], F32)
            gc = sbB("gc", [128, 32], F32)
            gb8 = sbB("gb8", [128, 96], F32)
            ge2 = sbB("ge2", [128, 4], F32)
            glog = sbB("glog", [128, 80], F32)

            ps_tb = stB.enter_context(nc.psum_tensor("ps_tb", [128, D], F32))
            ps_a = stB.enter_context(nc.psum_tensor("ps_a", [128, D], F32))
            ps_b = stB.enter_context(nc.psum_tensor("ps_b", [128, D], F32))
            ps_r = stB.enter_context(nc.psum_tensor("ps_r", [128, D], F32))

            PL.wait_ge(sv_x1, NT)
            PL.wait_ge(st_id, NT)
            PL.wait_ge(sc_qT, NT)
            PL.wait_ge(ss[0], SS0_TOT)
            PL.wait_ge(ss[1], SS1_TOT)
            PL.wait_ge(s_cy2[0], 16 * ((NT - 2) // 2))
            PL.wait_ge(s_cy2[1], 16 * (NT // 2))
            PL.dma_start(wg_f[:, :], wg_d[:, :]).then_inc(s_wg, 16)
            PL.dma_start(gbeb_f[:, :], gb_d[:, :]).then_inc(s_wg, 16)
            DVE.wait_ge(s_wg, 32)
            DVE.tensor_copy(wg_hi[:, :], wg_f[:, :])
            DVE.drain()
            DVE.tensor_copy(gbeb_b[0:1, 0:10], gbeb_f[0:1, :])
            DVE.drain()
            DVE.tensor_sub(gbeb_b[0:1, 10:20], gbeb_f[0:1, :], gbeb_b[0:1, 0:10])
            DVE.drain()
            DVE.engine_nop().then_inc(sv_g, 1)

            SP.wait_ge(sv_x1, NT)
            SP.wait_ge(st_id, NT)
            SP.wait_ge(sc_qT, NT)
            SP.wait_ge(ss[0], SS0_TOT)
            SP.wait_ge(ss[1], SS1_TOT)
            SP.wait_ge(s_cy2[0], 16 * ((NT - 2) // 2))
            SP.wait_ge(s_cy2[1], 16 * (NT // 2))

            out_r = out_d.ap().rearrange("(ch tl p) d -> ch p tl d", p=128, tl=TPC)
            out_t4 = out_d.ap().rearrange("(ch tl p) d -> ch tl p d", p=128, tl=TPC)
            x1r = x1s_d.ap().rearrange("(ch tl p) d -> ch p tl d", p=128, tl=TPC)
            acc3 = acc.ap().rearrange("p (tl d) -> p tl d", d=D)

            RPC = TPC + NE * TPC
            ctr = {"g": 1, "tb": 0, "gx": 0}
            g_done = {}
            pend_st = []
            gx_ln = {}
            ev_tile = {}
            g_hl = {}
            n_ab = 0
            n_r = 0
            for c in range(NCH):
                def s1_chain(cc2, i):
                    G = cc2 * TPC + i
                    r = G % 2
                    src2 = x1t[:, r * D:(r + 1) * D]
                    if G >= 2:
                        SP.wait_ge(sv_g, g_hl[G - 2])
                    SP.dma_start(src2, x1s_d.ap().rearrange("(n p) d -> n p d", p=128)[G]
                                 ).then_inc(s_x1t[r], 16)
                    DVE.wait_ge(s_x1t[r], 16 * (G // 2 + 1))
                    if G >= TPC:
                        DVE.wait_ge(st_tb, G - TPC + 1)
                    DVE.bn_stats(st2[:, 0:6], src2[:, 0:512])
                    DVE.bn_stats(st2[:, 6:12], src2[:, 512:1024])
                    DVE.drain()
                    DVE.bn_aggr(ag2[:, 2 * i:2 * i + 2], st2[:, :])
                    DVE.drain()
                    DVE.engine_nop().then_inc(sv_g, 1)
                    ctr["g"] += 1
                    ACT.wait_ge(sv_g, ctr["g"])
                    ACT.activation(rc2[:, 2 * i + 1:2 * i + 2], ag2[:, 2 * i + 1:2 * i + 2],
                                   ACTF.Ln, bias=eps_c[:, 0:1])
                    ACT.drain()
                    ACT.activation(rc2[:, 2 * i:2 * i + 1], rc2[:, 2 * i + 1:2 * i + 2],
                                   ACTF.Exp, scale=-0.5)
                    ACT.drain()
                    ACT.nop().then_inc(sc_gx, 1)
                    ctr["gx"] += 1
                    gx_ln[G] = ctr["gx"]
                    DVE.wait_ge(sc_gx, ctr["gx"])
                    DVE.tensor_scalar(hi_b[:, i * D:(i + 1) * D], src2, ag2[:, 2 * i:2 * i + 1],
                                      rc2[:, 2 * i:2 * i + 1], ALU.subtract, ALU.mult)
                    DVE.drain()
                    DVE.engine_nop().then_inc(sv_g, 1)
                    ctr["g"] += 1
                    g_hl[G] = ctr["g"]

                def s1_pe(i):
                    G = c * TPC + i
                    q = i % 2
                    tr_ps = ps_tb if (i % 2 == 0) else ps_a
                    PE.wait_ge(sv_g, g_hl[G])
                    if c > 0 and i == 0:
                        PE.wait_ge(sv_acc, c * RPC)
                        PE.wait_ge(sv_sT, 32 * c)
                        PE.wait_ge(sc_sil, 32 * c)
                    if ctr["tb"] >= 2:
                        PE.wait_ge(sc_ev, ctr["tb"] - 1)
                    for dc in range(8):
                        mi = PE.matmul(tr_ps[:, dc * 128:(dc + 1) * 128],
                                       hi_b[:, i * D + dc * 128: i * D + (dc + 1) * 128],
                                       id_b[:, :], start=True, stop=True)
                    mi.then_inc(st_tb, 1)
                    ctr["tb"] += 1
                    ACT.wait_ge(st_tb, ctr["tb"])
                    if i == 0:
                        ACT.wait_ge(st_ab, 32 * c)
                        ACT.wait_ge(st_g, c * TPC)
                    ACT.copy(
                        hiT.ap().rearrange("p (dc n) -> p dc n", dc=8)[:, :, i * 128:(i + 1) * 128],
                        tr_ps.ap().rearrange("p (dc n) -> p dc n", dc=8),
                    ).then_inc(sc_ev, 1)
                    ev_tile[G] = ctr["tb"]
                def s1_gate_mm(i):
                    G = c * TPC + i
                    w = ps_b[:, 10 * i:10 * i + 10]
                    PE.wait_ge(sc_ev, ev_tile[G])
                    if G == 0:
                        PE.wait_ge(s_wg, 32)
                        PE.wait_ge(sv_g, 1)
                    for dc in range(8):
                        PE.matmul(w,
                                  hiT[:, dc * CH + i * 128: dc * CH + (i + 1) * 128],
                                  wg_hi[:, dc * 10:(dc + 1) * 10],
                                  start=(dc == 0), stop=False)
                    PE.matmul(w, orow_b[0:1, :], gbeb_b[0:1, 0:10],
                              start=False, stop=False)
                    PE.matmul(w, orow_b[0:1, :], gbeb_b[0:1, 10:20],
                              start=False, stop=True).then_inc(st_g, 1)

                def s1_gate(i):
                    G = c * TPC + i
                    q = i % 2
                    gl = glog[:, 10 * i:10 * i + 10]
                    DVE.wait_ge(sc_gl, c + 1)
                    el = gl[:, 2:10]
                    gcq = gc[:, 16 * q:16 * q + 16]
                    gbq = gb8[:, 48 * q:48 * q + 48]
                    msk, eex = gbq[:, 0:8], gbq[:, 8:16]
                    p8, t8 = gbq[:, 16:24], gbq[:, 24:32]
                    p2, k8 = gbq[:, 32:40], gbq[:, 40:48]
                    DVE.tensor_tensor(gcq[:, 0:1], gl[:, 1:2], gl[:, 0:1], ALU.is_gt)
                    DVE.reduce_max(gcq[:, 1:2], gl[:, 0:2], axis=AX.X)
                    DVE.drain()
                    DVE.tensor_scalar(gcq[:, 1:2], gcq[:, 1:2], -1.0, None, ALU.mult)
                    DVE.tensor_scalar(gcq[:, 4:5], gcq[:, 0:1], -BIG, None, ALU.mult)
                    DVE.tensor_scalar(gcq[:, 5:6], gcq[:, 0:1], 1.0, BIG, ALU.subtract, ALU.mult)
                    DVE.drain()
                    DVE.tensor_scalar(msk[:, 0:4], el[:, 0:4], gcq[:, 4:5], None, ALU.add)
                    DVE.tensor_scalar(msk[:, 4:8], el[:, 4:8], gcq[:, 5:6], None, ALU.add)
                    DVE.drain()
                    DVE.reduce_max(gcq[:, 6:7], msk[:, :], axis=AX.X)
                    DVE.drain()
                    DVE.tensor_scalar(gcq[:, 6:7], gcq[:, 6:7], -1.0, None, ALU.mult)
                    DVE.drain()
                    DVE.engine_nop().then_inc(sv_g, 1)
                    ctr["g"] += 1
                    ACT.wait_ge(sv_g, ctr["g"])
                    if G >= 1:
                        ACT.wait_ge(sv_se, G)
                    ACT.activation(ge2[:, 2 * q:2 * q + 2], gl[:, 0:2], ACTF.Exp,
                                   bias=gcq[:, 1:2])
                    ACT.drain()
                    ACT.activation(eex[:, :], msk[:, :], ACTF.Exp, bias=gcq[:, 6:7])
                    ACT.drain()
                    ACT.nop().then_inc(sc_gx, 2)
                    ctr["gx"] += 2
                    DVE.wait_ge(sc_gx, ctr["gx"])
                    DVE.reduce_sum(gcq[:, 2:3], ge2[:, 2 * q:2 * q + 2], axis=AX.X)
                    DVE.reduce_sum(gcq[:, 7:8], eex[:, :], axis=AX.X)
                    DVE.drain()
                    DVE.reciprocal(gcq[:, 3:4], gcq[:, 2:3])
                    DVE.reciprocal(gcq[:, 8:9], gcq[:, 7:8])
                    DVE.drain()
                    DVE.tensor_mul(gcq[:, 9:10], gcq[:, 8:9], gcq[:, 3:4])
                    DVE.drain()
                    DVE.tensor_scalar(p8[:, :], eex[:, :], gcq[:, 9:10], None, ALU.mult)
                    DVE.drain()
                    DVE.reduce_max(gcq[:, 10:11], p8[:, :], axis=AX.X)
                    DVE.drain()
                    DVE.tensor_scalar(t8[:, :], p8[:, :], gcq[:, 10:11], None, ALU.is_ge)
                    DVE.drain()
                    DVE.scalar_tensor_tensor(p2[:, :], t8[:, :], -BIG, p8[:, :],
                                             ALU.mult, ALU.add)
                    DVE.drain()
                    DVE.reduce_max(gcq[:, 11:12], p2[:, :], axis=AX.X)
                    DVE.drain()
                    DVE.tensor_scalar(k8[:, :], p8[:, :], gcq[:, 11:12], None, ALU.is_ge)
                    DVE.drain()
                    DVE.tensor_mul(wse[:, i * NE:(i + 1) * NE], k8[:, :], p8[:, :])
                    DVE.drain()
                    DVE.engine_nop().then_inc(sv_se, 1)

                def wout_load(cc):
                    kw2 = cc * 9
                    bw2 = kw2 % 2
                    if kw2 == 0:
                        PL.wait_ge(s_cvt, CVT)
                    if kw2 >= 2:
                        a2, j2 = divmod(kw2 - 2, 9)
                        PL.wait_ge(st_r, a2 * RPC + (j2 + 1) * TPC)
                    for h2 in range(2):
                        PL.dma_start(w2_b[bw2][:, h2 * 8 * D:(h2 + 1) * 8 * D],
                                     w2s_d.ap()[0, :, h2 * 8 * D:(h2 + 1) * 8 * D]
                                     ).then_inc(s_w2[bw2], 16)

                if c == 0:
                    for i in range(TPC):
                        s1_chain(0, i)
                for i in range(TPC):
                    s1_pe(i)
                for i in range(TPC):
                    s1_gate_mm(i)
                ACT.wait_ge(st_g, TPC * (c + 1))
                if c > 0:
                    ACT.wait_ge(sv_g, g_done[c - 1])
                ACT.copy(glog[:, :], ps_b[:, 0:10 * TPC]).then_inc(sc_gl, 1)
                for i in range(TPC):
                    s1_gate(i)
                g_done[c] = ctr["g"]

                for kind in range(2):
                    wa_d, wb_d2 = (wabs_d.ap()[2 * kind], wabs_d.ap()[2 * kind + 1])
                    for hc in range(16):
                        gidx = (c * 2 + kind) * 16 + hc
                        bi = gidx % 2
                        pa, pb = (ps_a, ps_b) if (gidx % 2 == 0) else (ps_tb, ps_r)
                        sil = sil_b[:, (gidx % 2) * D:(gidx % 2) * D + CH]
                        if gidx == 0:
                            SP.wait_ge(s_cvt, CVT)
                            PL.wait_ge(s_cvt, CVT)
                        if gidx >= 2:
                            SP.wait_ge(sv_sT, gidx - 1)
                            PL.wait_ge(sv_sT, gidx - 1)
                        SP.dma_start(wab_b[bi][:, 0:D], wa_d[hc]).then_inc(s_wa[bi], 16)
                        PL.dma_start(wab_b[bi][:, D:2 * D], wb_d2[hc]).then_inc(s_wb[bi], 16)
                        PE.wait_ge(s_wa[bi], 16 * (gidx // 2 + 1))
                        if gidx >= 2:
                            PE.wait_ge(sv_sT, gidx - 1)
                        if kind == 0 and hc == 0:
                            PE.wait_ge(sc_gl, c + 1)
                        if kind == 0 and hc <= 1:
                            PE.wait_ge(sc_ev, TPC * (c + 1))
                        if kind == 1 and hc <= 1:
                            PE.wait_ge(sv_acc, c * RPC + TPC)
                        WL = min(512, CH)
                        for dc in range(8):
                            for h in range(CH // WL):
                                PE.matmul(pa[:, WL * h:WL * (h + 1)],
                                          wab_b[bi][:, dc * 128:(dc + 1) * 128],
                                          hiT[:, dc * CH + WL * h: dc * CH + WL * (h + 1)],
                                          start=(dc == 0), stop=(dc == 7))
                        PE.wait_ge(s_wb[bi], 16 * (gidx // 2 + 1))
                        for dc in range(8):
                            for h in range(CH // WL):
                                mi = PE.matmul(pb[:, WL * h:WL * (h + 1)],
                                               wab_b[bi][:, D + dc * 128: D + (dc + 1) * 128],
                                               hiT[:, dc * CH + WL * h: dc * CH + WL * (h + 1)],
                                               start=(dc == 0), stop=(dc == 7))
                        mi.then_inc(st_ab, 1)
                        n_ab += 1
                        ACT.wait_ge(st_ab, n_ab)
                        ACT.activation(sil, pa[:, 0:CH], ACTF.Silu)
                        ACT.drain()
                        ACT.nop().then_inc(sc_sil, 1)
                        DVE.wait_ge(sc_sil, n_ab)
                        if hc == 0:
                            if kind == 0 and c > 0:
                                DVE.wait_ge(st_r, c * RPC)
                            if kind == 1:
                                DVE.wait_ge(st_r, c * RPC + TPC)
                        DVE.tensor_mul(hsT[:, hc * CH:(hc + 1) * CH], sil, pb[:, 0:CH])
                        DVE.drain()
                        DVE.engine_nop().then_inc(sv_sT, 1)
                    if kind == 0:
                        if c > 0:
                            SP.wait_ge(s_out, 128 * c)
                            SP.wait_ge(sv_fin, c)
                        SP.dma_start(acc3, x1r[c], ).then_inc(s_ac, 16)
                        kw = c * 9
                        bw = kw % 2
                        if c == 0:
                            wout_load(0)
                        for i in range(TPC):
                            pr = ps_r if (n_r % 2 == 0) else ps_tb
                            PE.wait_ge(sv_sT, c * 32 + 16)
                            PE.wait_ge(s_w2[bw], 32 * (kw // 2 + 1))
                            if n_r >= 2:
                                PE.wait_ge(sv_acc, n_r - 1)
                            for hc in range(16):
                                for h in range(2):
                                    mi = PE.matmul(pr[:, 512 * h:512 * (h + 1)],
                                                   hsT[:, hc * CH + i * 128: hc * CH + (i + 1) * 128],
                                                   w2_b[bw][:, hc * D + 512 * h: hc * D + 512 * (h + 1)],
                                                   start=(hc == 0), stop=(hc == 15))
                            mi.then_inc(st_r, 1)
                            n_r += 1
                            DVE.wait_ge(st_r, n_r)
                            DVE.wait_ge(s_ac, 16 * (c + 1))
                            DVE.tensor_add(acc[:, i * D:(i + 1) * D],
                                           acc[:, i * D:(i + 1) * D], pr[:, :])
                            DVE.drain()
                            DVE.engine_nop().then_inc(sv_acc, 1)
                for e in range(NE):
                    if c + 1 < NCH and e < TPC:
                        s1_chain(c + 1, e)
                    gw = c * 9 + 1 + e
                    bi = gw % 2
                    if gw >= 2:
                        cc, jj = divmod(gw - 2, 9)
                        PL.wait_ge(st_r, cc * RPC + (jj + 1) * TPC)
                    for h2 in range(2):
                        PL.dma_start(w2_b[bi][:, h2 * 8 * D:(h2 + 1) * 8 * D],
                                     w2s_d.ap()[1 + e, :, h2 * 8 * D:(h2 + 1) * 8 * D]
                                     ).then_inc(s_w2[bi], 16)
                    for i in range(TPC):
                        pr = ps_r if (n_r % 2 == 0) else ps_tb
                        PE.wait_ge(s_w2[bi], 32 * (gw // 2 + 1))
                        if n_r >= 2:
                            PE.wait_ge(sv_acc, n_r - 1)
                        if e == 0 and i == 0:
                            PE.wait_ge(sv_sT, c * 32 + 32)
                        for hc in range(16):
                            for h in range(2):
                                mi = PE.matmul(pr[:, 512 * h:512 * (h + 1)],
                                               hsT[:, hc * CH + i * 128: hc * CH + (i + 1) * 128],
                                               w2_b[bi][:, hc * D + 512 * h: hc * D + 512 * (h + 1)],
                                               start=(hc == 0), stop=(hc == 15))
                        mi.then_inc(st_r, 1)
                        n_r += 1
                        DVE.wait_ge(st_r, n_r)
                        DVE.wait_ge(sv_se, c * TPC + i + 1)
                        DVE.scalar_tensor_tensor(acc[:, i * D:(i + 1) * D], pr[:, :],
                                                 wse[:, i * NE + e: i * NE + e + 1],
                                                 acc[:, i * D:(i + 1) * D],
                                                 ALU.mult, ALU.add)
                        DVE.drain()
                        DVE.engine_nop().then_inc(sv_acc, 1)
                        if e == NE - 1:
                            SP.wait_ge(sv_acc, n_r)
                            SP.dma_start(out_t4[c, i], acc[:, i * D:(i + 1) * D]
                                         ).then_inc(s_out, 16)
                if c + 1 < NCH:
                    wout_load(c + 1)
                DVE.engine_nop().then_inc(sv_fin, 1)
            SP.wait_ge(s_out, 128 * NCH)
    return nc




def kernel(L0_attn_x, L0_attn_Wk, L0_attn_bk, L0_attn_Wv, L0_attn_bv,
           L0_attn_Wq, L0_attn_bq, L0_ffn_w_in, L0_ffn_w_out, L0_ffn_w1,
           L0_ffn_w2, L0_ffn_wg, L0_ffn_we, L0_ffn_gb, L0_ffn_eb):
    Bsz, S, _ = L0_attn_x.shape
    T = S // 2
    xf = np.ascontiguousarray(np.asarray(L0_attn_x, dtype=np.float32)).reshape(Bsz * S, D)
    consts = _make_consts()

    def colmaj(v):
        return np.ascontiguousarray(np.asarray(v, dtype=np.float32).reshape(8, 128).T)

    w_in = np.asarray(L0_ffn_w_in, dtype=np.float32)
    w1 = np.asarray(L0_ffn_w1, dtype=np.float32)
    w_out = np.asarray(L0_ffn_w_out, dtype=np.float32)
    w2 = np.asarray(L0_ffn_w2, dtype=np.float32)
    wg = np.asarray(L0_ffn_wg, dtype=np.float32)
    we = np.asarray(L0_ffn_we, dtype=np.float32)
    wgT = np.concatenate([wg, we], axis=0).T

    shared_w = {
        "Wk": _chunkmaj(np.asarray(L0_attn_Wk, dtype=np.float32)),
        "Wv": _chunkmaj(np.asarray(L0_attn_Wv, dtype=np.float32)),
        "Wq": _chunkmaj(np.asarray(L0_attn_Wq, dtype=np.float32)),
        "bk": colmaj(L0_attn_bk), "bv": colmaj(L0_attn_bv), "bq": colmaj(L0_attn_bq),
        "w_inT_a": _pieces_16(w_in[0:HID]),
        "w_inT_b": _pieces_16(w_in[HID:2 * HID]),
        "w1T_a": _pieces_16(w1[0:HID]),
        "w1T_b": _pieces_16(w1[HID:2 * HID]),
        "w_outT": _chunkmaj(np.ascontiguousarray(w_out.T)),
        "w2T": np.ascontiguousarray(np.stack(
            [_chunkmaj(np.ascontiguousarray(w2[e].T)) for e in range(NE)])),
        "wgT": _chunkmaj(np.ascontiguousarray(wgT)),
        "gbeb": np.ascontiguousarray(np.concatenate(
            [np.asarray(L0_ffn_gb, np.float32),
             np.asarray(L0_ffn_eb, np.float32)]).reshape(1, 10)),
        **consts,
    }

    nc = build_nc(T=T, TP=T, CH=1024)
    in_maps = []
    for c in range(NCORES):
        bidx, half = c // 2, c % 2
        base = bidx * S
        m = dict(shared_w)
        m["x"] = np.ascontiguousarray(xf[base + half * T: base + (half + 1) * T])
        m["xp"] = np.ascontiguousarray(xf[base: base + T])
        m["mask"] = np.full((1, 1), float(half), dtype=np.float32)
        in_maps.append(m)
    res = run_bass_kernel_spmd(nc, in_maps, core_ids=list(range(NCORES)))
    out = np.empty((Bsz * S, D), dtype=np.float32)
    for c in range(NCORES):
        bidx, half = c // 2, c % 2
        base = bidx * S
        out[base + half * T: base + (half + 1) * T] = res.results[c]["out"]
    return out.reshape(Bsz, S, D)


# version 30
# speedup vs baseline: 1.0185x; 1.0185x over previous
import math
from contextlib import ExitStack

import numpy as np
import ml_dtypes

import concourse.bass as bass
import concourse.mybir as mybir
from concourse.bass_utils import run_bass_kernel_spmd

F32 = mybir.dt.float32
BF16 = mybir.dt.bfloat16
ALU = mybir.AluOpType
ACTF = mybir.ActivationFunctionType
AX = mybir.AxisListType
BF = ml_dtypes.bfloat16

D = 1024
HID = 2048
NE = 8
NCORES = 8
BIG = 1e30



def _chunkmaj(a, p=128):
    R, C = a.shape
    return np.ascontiguousarray(
        a.reshape(R // p, p, C).transpose(1, 0, 2).reshape(p, (R // p) * C)
    )


def _make_consts():
    N = D
    j = np.arange(N, dtype=np.float64)
    f = np.arange(512, dtype=np.float64)
    ang = 2.0 * np.pi * np.outer(j, f) / N
    cosm = np.cos(ang)
    sinm = np.sin(ang)
    alt = np.where(np.arange(N) % 2 == 0, 1.0, -1.0)

    ft = np.zeros((N, N), dtype=np.float64)
    ft[:, 0:512] = cosm
    ft[:, 512] = alt
    ft[:, 513:1024] = -sinm[:, 1:512]
    s = np.ones((N,), dtype=np.float64)
    s[1:512] = np.sqrt(2.0)
    s[513:1024] = np.sqrt(2.0)
    ft_k = ft * s[None, :]
    ft_vq = ft
    ft_q = ft.copy()
    ft_q[:, 513:1024] = -ft_q[:, 513:1024]

    n_ = np.arange(N, dtype=np.float64)
    angn = 2.0 * np.pi * np.outer(f, n_) / N
    g = np.zeros((N, N), dtype=np.float64)
    c = 1.0 / (N * math.sqrt(D))
    g[0, :] = c
    g[1:512, :] = math.sqrt(2.0) * np.cos(angn[1:512, :]) * c
    g[512, :] = alt * c
    g[513:1024, :] = -math.sqrt(2.0) * np.sin(angn[1:512, :]) * c

    return {
        "ftk": _chunkmaj(ft_k.astype(np.float32)),
        "ftvq": _chunkmaj(ft_vq.astype(np.float32)),
        "ftq": _chunkmaj(ft_q.astype(np.float32)),
        "gmat": _chunkmaj(g.astype(np.float32)),
        "ident": np.eye(128, dtype=np.float32),
        "ltri": np.triu(np.ones((128, 128), dtype=np.float32)),
        "onesrow": np.ones((1, 128), dtype=np.float32),
        "onescol": np.ones((128, 1), dtype=np.float32),
    }


def _pieces_16(w):
    a = w.reshape(16, 128, 8, 128)
    return np.ascontiguousarray(a.transpose(0, 3, 2, 1).reshape(16, 128, 1024))




def build_nc(T=4096, TP=4096, CH=1024):
    assert T % CH == 0 and CH % 128 == 0 and TP % 128 == 0
    NT = T // 128
    NPT = TP // 128
    NCH = T // CH
    TPC = CH // 128

    nc = bass.Bass("TRN2", target_bir_lowering=False)

    x_d = nc.dram_tensor("x", [T, D], F32, kind="ExternalInput")
    xp_d = nc.dram_tensor("xp", [TP, D], F32, kind="ExternalInput")
    mask_d = nc.dram_tensor("mask", [1, 1], F32, kind="ExternalInput")
    wk_d = nc.dram_tensor("Wk", [128, 8 * D], F32, kind="ExternalInput")
    wv_d = nc.dram_tensor("Wv", [128, 8 * D], F32, kind="ExternalInput")
    wq_d = nc.dram_tensor("Wq", [128, 8 * D], F32, kind="ExternalInput")
    bk_d = nc.dram_tensor("bk", [128, 8], F32, kind="ExternalInput")
    bv_d = nc.dram_tensor("bv", [128, 8], F32, kind="ExternalInput")
    bq_d = nc.dram_tensor("bq", [128, 8], F32, kind="ExternalInput")
    ftk_d = nc.dram_tensor("ftk", [128, 8 * D], F32, kind="ExternalInput")
    ftvq_d = nc.dram_tensor("ftvq", [128, 8 * D], F32, kind="ExternalInput")
    ftq_d = nc.dram_tensor("ftq", [128, 8 * D], F32, kind="ExternalInput")
    gm_d = nc.dram_tensor("gmat", [128, 8 * D], F32, kind="ExternalInput")
    id_d = nc.dram_tensor("ident", [128, 128], F32, kind="ExternalInput")
    lt_d = nc.dram_tensor("ltri", [128, 128], F32, kind="ExternalInput")
    or_d = nc.dram_tensor("onesrow", [1, 128], F32, kind="ExternalInput")
    oc_d = nc.dram_tensor("onescol", [128, 1], F32, kind="ExternalInput")
    wia_d = nc.dram_tensor("w_inT_a", [16, 128, D], F32, kind="ExternalInput")
    wib_d = nc.dram_tensor("w_inT_b", [16, 128, D], F32, kind="ExternalInput")
    w1a_d = nc.dram_tensor("w1T_a", [16, 128, D], F32, kind="ExternalInput")
    w1b_d = nc.dram_tensor("w1T_b", [16, 128, D], F32, kind="ExternalInput")
    wo_d = nc.dram_tensor("w_outT", [128, 16 * D], F32, kind="ExternalInput")
    w2_d = nc.dram_tensor("w2T", [NE, 128, 16 * D], F32, kind="ExternalInput")
    wg_d = nc.dram_tensor("wgT", [128, 80], F32, kind="ExternalInput")
    gb_d = nc.dram_tensor("gbeb", [1, 10], F32, kind="ExternalInput")
    out_d = nc.dram_tensor("out", [T, D], F32, kind="ExternalOutput")
    x1s_d = nc.dram_tensor("x1s", [T, D], F32)
    wabs_d = nc.dram_tensor("wabs", [4, 16, 128, D], BF16)
    w2s_d = nc.dram_tensor("w2s", [NE + 1, 128, 16 * D], BF16)

    PE, DVE, ACT, SP, PL = nc.tensor, nc.vector, nc.scalar, nc.sync, nc.gpsimd

    with ExitStack() as glob:
        sem = lambda name: glob.enter_context(nc.semaphore(name))
        sb = lambda name, shape, dt: glob.enter_context(nc.sbuf_tensor(name, shape, dt))

        id_b = sb("id_b", [128, 128], BF16)
        lt_b = sb("lt_b", [128, 128], BF16)
        orow_b = sb("orow_b", [1, 128], BF16)
        ocol_b = sb("ocol_b", [128, 1], BF16)
        mask_f = sb("mask_f", [1, 1], F32)
        eps_c = sb("eps_c", [128, 1], F32)
        cry_b = sb("cry_b", [1, 2 * D], BF16)

        s_cst = sem("s_cst")
        s_cvt = sem("s_cvt")
        PL.dma_start(id_b[:, :], id_d[:, :]).then_inc(s_cst, 16)
        PL.dma_start(lt_b[:, :], lt_d[:, :]).then_inc(s_cst, 16)
        PL.dma_start(orow_b[:, :], or_d[:, :]).then_inc(s_cst, 16)
        PL.dma_start(ocol_b[:, :], oc_d[:, :]).then_inc(s_cst, 16)
        PL.dma_start(mask_f[:, :], mask_d[:, :]).then_inc(s_cst, 16)
        CST = 80
        s_ini = sem("s_ini")
        DVE.memset(eps_c[:, :], 1e-5)
        DVE.drain()
        DVE.engine_nop().then_inc(s_ini, 1)
        ACT.wait_ge(s_ini, 1)

        s_ft = sem("s_ft")
        s_wr = [sem(f"s_wr{i}") for i in range(2)]
        s_bc = sem("s_bc")
        sx = [sem(f"sx{i}") for i in range(6)]
        ss = [sem(f"ss{i}") for i in range(2)]
        st_f = sem("st_f")
        sc_f = sem("sc_f")
        sv_td = sem("sv_td")
        sv_st = sem("sv_st")
        sc_ln = sem("sc_ln")
        sv_ln = sem("sv_ln")
        st_xT = sem("st_xT")
        sc_xT = sem("sc_xT")
        st_k = sem("st_k")
        st_v = sem("st_v")
        st_q = sem("st_q")
        sc_k = sem("sc_k")
        sc_v = sem("sc_v")
        sc_cq = sem("sc_cq")
        st_cy = sem("st_cy")
        s_cy2 = [sem(f"s_cy2{i}") for i in range(2)]
        sv_kv = sem("sv_kv")
        st_cs = sem("st_cs")
        sv_cr = sem("sv_cr")
        sv_cy = sem("sv_cy")
        st_m = sem("st_m")
        sc_m = sem("sc_m")
        sv_m = sem("sv_m")
        st_cm = sem("st_cm")
        sv_q1 = sem("sv_q1")
        sc_nm = sem("sc_nm")
        sv_qn = sem("sv_qn")
        st_qT = sem("st_qT")
        sc_qT = sem("sc_qT")
        st_id = sem("st_id")
        sv_x1 = sem("sv_x1")

        with ExitStack() as stA:
            sbA = lambda name, shape, dt: stA.enter_context(nc.sbuf_tensor(name, shape, dt))
            wtk_b = sbA("wtk_b", [128, 8 * D], BF16)
            wtv_b = sbA("wtv_b", [128, 8 * D], BF16)
            wtq_b = sbA("wtq_b", [128, 8 * D], BF16)
            biasP = sbA("biasP", [1, 3 * D], BF16)

            ps_tr = stA.enter_context(nc.psum_tensor("ps_tr", [128, D], F32))
            ps_kv = stA.enter_context(nc.psum_tensor("ps_kv", [128, D], F32))
            ps_q = stA.enter_context(nc.psum_tensor("ps_q", [128, D], F32))

            xf = sbA("xf", [128, 4 * D], F32)
            xln_b = sbA("xln_b", [128, 2 * D], BF16)
            xT_b = sbA("xT_b", [128, 2 * D], BF16)
            stats = sbA("stats", [128, 12], F32)
            aggr = sbA("aggr", [128, 4], F32)
            rcol = sbA("rcol", [128, 4], F32)

            with ExitStack() as stF:
                sbF = lambda name, shape, dt: stF.enter_context(nc.sbuf_tensor(name, shape, dt))
                ps_cy = stF.enter_context(nc.psum_tensor("ps_cy", [1, D], F32))
                ftb = sbF("ftb", [128, 8 * D], BF16)
                wraw = sbF("wraw", [128, 8 * D], BF16)
                bcol = sbF("bcol", [128, 24], BF16)
                xln_all = sbF("xln_all", [128, NPT * D], BF16)
                m_sb = sbF("m_sb", [128, 8 * D], BF16)
                hh = xT_b
                xbr_row = sbF("xbr_row", [1, D], BF16)
                xbr_col = sbF("xbr_col", [128, 8], BF16)
                axcx = sbF("axcx", [1, 2 * D], BF16)
                s_sb = sbF("s_sb", [1, 4 * 512], BF16)
                tr1 = sbF("tr1", [1, 2 * 512], BF16)
                ps_cs = ps_cy

                PL.dma_start(bcol[:, 0:8], bk_d[:, :]).then_inc(s_bc, 16)
                PL.dma_start(bcol[:, 8:16], bv_d[:, :]).then_inc(s_bc, 16)
                PL.dma_start(bcol[:, 16:24], bq_d[:, :]).then_inc(s_bc, 16)

                nfold = 0
                for pi, (w_d, ft_d, wt_b) in enumerate(
                        [(wk_d, ftk_d, wtk_b), (wv_d, ftvq_d, wtv_b),
                         (wq_d, ftq_d, wtq_b)]):
                    if pi >= 1:
                        PL.wait_ge(st_f, 9 * pi)
                    PL.dma_start(ftb[:, :], ft_d[:, :]).then_inc(s_ft, 16)
                    PL.dma_start(wraw[:, :], w_d[:, :]).then_inc(s_ft, 16)
                    PE.wait_ge(s_ft, 32 * (pi + 1))
                    if pi == 0:
                        PE.wait_ge(s_cst, CST)
                        PE.wait_ge(s_bc, 48)
                    wbuf = wraw
                    for dc in range(8):
                        if nfold >= 2:
                            PE.wait_ge(sc_f, nfold - 1)
                        ps = ps_tr if (nfold % 2 == 0) else ps_kv
                        for jc in range(8):
                            for h in range(2):
                                mi = PE.matmul(
                                    ps[:, 512 * h:512 * (h + 1)],
                                    wbuf[:, jc * D + dc * 128: jc * D + (dc + 1) * 128],
                                    ftb[:, jc * D + 512 * h: jc * D + 512 * (h + 1)],
                                    start=(jc == 0), stop=(jc == 7))
                        mi.then_inc(st_f, 1)
                        nfold += 1
                        ACT.wait_ge(st_f, nfold)
                        ACT.copy(wt_b[:, dc * D:(dc + 1) * D], ps[:, :]).then_inc(sc_f, 1)
                    if pi > 0:
                        PE.wait_ge(sc_f, nfold)
                    for jc in range(8):
                        for h in range(2):
                            mi = PE.matmul(
                                ps_cs[:, 512 * h:512 * (h + 1)],
                                bcol[:, 8 * pi + jc: 8 * pi + jc + 1],
                                ftb[:, jc * D + 512 * h: jc * D + 512 * (h + 1)],
                                start=(jc == 0), stop=(jc == 7))
                    mi.then_inc(st_f, 1)
                    nfold += 1
                    ACT.wait_ge(st_f, nfold)
                    ACT.copy(biasP[0:1, pi * D:(pi + 1) * D], ps_cs[0:1, :]).then_inc(sc_f, 1)
                FOLD = nfold
                CVT = 16 * 13

                x_t = x_d.ap().rearrange("(n p) d -> n p d", p=128)
                xp_t = xp_d.ap().rearrange("(n p) d -> n p d", p=128)
                x1s_t = x1s_d.ap().rearrange("(n p) d -> n p d", p=128)
                GTOT = NPT + NT

                def xsrc(g):
                    return xf[:, (g % 4) * D:(g % 4 + 1) * D]

                def sp_load(g):
                    if g >= 4:
                        SP.wait_ge(sv_td, g - 3)
                    ap = xp_t[g] if g < NPT else x_t[g - NPT]
                    SP.dma_start(xsrc(g), ap).then_inc(sx[g % 4], 16)

                def d_stats(g):
                    DVE.wait_ge(sx[g % 4], 16 * (g // 4 + 1))
                    DVE.bn_stats(stats[:, 0:6], xsrc(g)[:, 0:512])
                    DVE.bn_stats(stats[:, 6:12], xsrc(g)[:, 512:1024])
                    DVE.drain()
                    DVE.bn_aggr(aggr[:, 2 * (g % 2):2 * (g % 2) + 2], stats[:, :])
                    DVE.drain()
                    DVE.engine_nop().then_inc(sv_st, 1)

                def a_lnexp(g):
                    r = g % 2
                    ACT.wait_ge(sv_st, g + 1)
                    if g >= 2:
                        ACT.wait_ge(sv_ln, g - 1)
                    ACT.activation(rcol[:, 2 * r + 1:2 * r + 2], aggr[:, 2 * r + 1:2 * r + 2],
                                   ACTF.Ln, bias=eps_c[:, 0:1])
                    ACT.drain()
                    ACT.activation(rcol[:, 2 * r:2 * r + 1], rcol[:, 2 * r + 1:2 * r + 2],
                                   ACTF.Exp, scale=-0.5)
                    ACT.drain()
                    ACT.nop().then_inc(sc_ln, 1)

                def d_apply(g):
                    r = g % 2
                    DVE.wait_ge(sc_ln, g + 1)
                    if g < NPT:
                        dst = xln_all[:, g * D:(g + 1) * D]
                    else:
                        t = g - NPT
                        if t >= 2:
                            DVE.wait_ge(st_xT, t - 1)
                        dst = xln_b[:, r * D:(r + 1) * D]
                    DVE.tensor_scalar(dst, xsrc(g),
                                      aggr[:, 2 * r:2 * r + 1], rcol[:, 2 * r:2 * r + 1],
                                      ALU.subtract, ALU.mult)
                    DVE.drain()
                    DVE.engine_nop().then_inc(sv_ln, 1)
                    if g < NPT:
                        DVE.engine_nop().then_inc(sv_td, 1)

                def ln_chain(g):
                    d_stats(g)
                    a_lnexp(g)
                    d_apply(g)

                def p_xbar(pt):
                    PE.wait_ge(sv_ln, pt + 1)
                    for h in range(2):
                        mi = PE.matmul(ps_cy[0:1, 512 * h:512 * (h + 1)], ocol_b[:, :],
                                       xln_all[:, pt * D + 512 * h: pt * D + 512 * (h + 1)],
                                       start=(pt == 0), stop=(pt == NPT - 1))
                    mi.then_inc(st_cy, 1)

                def p_trX(t):
                    PE.wait_ge(sv_ln, NPT + t + 1)
                    if t == 0:
                        PE.wait_ge(sv_m, MV_TOT[0])
                    else:
                        PE.wait_ge(sc_k, t)
                    xl = xln_b[:, (t % 2) * D:(t % 2 + 1) * D]
                    for cc in range(8):
                        mi = PE.matmul(ps_kv[:, cc * 128:(cc + 1) * 128],
                                       xl[:, cc * 128:(cc + 1) * 128], id_b[:, :],
                                       start=True, stop=True)
                    mi.then_inc(st_xT, 1)

                def a_evacX(t):
                    ACT.wait_ge(st_xT, t + 1)
                    if t >= 2:
                        ACT.wait_ge(st_v, t - 1)
                    ACT.copy(xT_b[:, (t % 2) * D:(t % 2 + 1) * D], ps_kv[:, :]).then_inc(sc_xT, 1)

                def _pj(ps, xt, wt, dc, h):
                    return PE.matmul(ps[:, 512 * h:512 * (h + 1)],
                                     xt[:, dc * 128:(dc + 1) * 128],
                                     wt[:, dc * D + 512 * h: dc * D + 512 * (h + 1)],
                                     start=(dc == 0), stop=False)

                def _pjbias(ps, bi, stop=True):
                    for h in range(2):
                        mi = PE.matmul(ps[:, 512 * h:512 * (h + 1)], orow_b[0:1, :],
                                       biasP[0:1, bi * D + 512 * h: bi * D + 512 * (h + 1)],
                                       start=False, stop=stop)
                    return mi

                def p_projK(t):
                    PE.wait_ge(sc_xT, t + 1)
                    xt = xT_b[:, (t % 2) * D:(t % 2 + 1) * D]
                    for dc in range(8):
                        for h in range(2):
                            _pj(ps_kv, xt, wtk_b, dc, h)
                    _pjbias(ps_kv, 0).then_inc(st_k, 1)

                def p_projQ(t):
                    if t == 0:
                        PE.wait_ge(sc_m, SCM_TOT[0])
                    else:
                        PE.wait_ge(sc_cq, t)
                    xt = xT_b[:, (t % 2) * D:(t % 2 + 1) * D]
                    for dc in range(8):
                        for h in range(2):
                            _pj(ps_q, xt, wtq_b, dc, h)
                    _pjbias(ps_q, 2).then_inc(st_q, 1)

                def a_evacK(t):
                    ACT.wait_ge(st_k, t + 1)
                    if t >= 1:
                        ACT.wait_ge(sv_kv, t)
                    ACT.copy(kvs_b[:, 0:D], ps_kv[:, :]).then_inc(sc_k, 1)

                def p_projV(t):
                    if t == 1:
                        PE.wait_ge(sc_v, 1)
                    elif t >= 2:
                        PE.wait_ge(sv_x1, t - 1)
                    xt = xT_b[:, (t % 2) * D:(t % 2 + 1) * D]
                    for dc in range(8):
                        for h in range(2):
                            _pj(ps_id, xt, wtv_b, dc, h)
                    _pjbias(ps_id, 1).then_inc(st_v, 1)

                def a_evacV(t):
                    ACT.wait_ge(st_v, t + 1)
                    ACT.copy(kvs_b[:, D:2 * D], ps_id[:, :]).then_inc(sc_v, 1)

                def cmul(dst_ap, pa, pb):
                    t1, t2 = prod[:, 0:512], prod[:, 512:1024]
                    t3, t4 = prod[:, 1024:1536], prod[:, 1536:2048]
                    DVE.tensor_mul(t1, pa[:, 0:512], pb[:, 0:512])
                    DVE.tensor_mul(t2, pa[:, 512:1024], pb[:, 512:1024])
                    DVE.tensor_mul(t3, pa[:, 0:512], pb[:, 512:1024])
                    DVE.tensor_mul(t4, pa[:, 512:1024], pb[:, 0:512])
                    DVE.drain()
                    DVE.tensor_sub(dst_ap[:, 1:512], t1[:, 1:512], t2[:, 1:512])
                    DVE.tensor_copy(dst_ap[:, 0:1], t1[:, 0:1])
                    DVE.tensor_add(dst_ap[:, 513:1024], t3[:, 1:512], t4[:, 1:512])
                    DVE.tensor_copy(dst_ap[:, 512:513], t2[:, 0:1])
                    DVE.drain()

                def d_products(t):
                    DVE.wait_ge(sc_v, t + 1)
                    if t >= 2:
                        DVE.wait_ge(st_cm, t - 1)
                    cmul(kv_b[:, (t % 2) * D:(t % 2 + 1) * D],
                         kvs_b[:, 0:D], kvs_b[:, D:2 * D])
                    DVE.engine_nop().then_inc(sv_kv, 1)

                def p_cum(t):
                    PE.wait_ge(sv_kv, t + 1)
                    if t == 0:
                        PE.wait_ge(sv_cr, 1)
                    else:
                        PE.wait_ge(s_cy2[t % 2], 16 * ((t + 1) // 2))
                        PE.wait_ge(sc_qT, t)
                    cry = cry_b[0:1, (t % 2) * D:(t % 2 + 1) * D]
                    kv = kv_b[:, (t % 2) * D:(t % 2 + 1) * D]
                    for h in range(2):
                        PE.matmul(ps_tr[:, 512 * h:512 * (h + 1)], orow_b[0:1, :],
                                  cry[0:1, 512 * h:512 * (h + 1)], start=True, stop=False)
                        mi = PE.matmul(ps_tr[:, 512 * h:512 * (h + 1)], lt_b[:, :],
                                       kv[:, 512 * h:512 * (h + 1)], start=False, stop=True)
                    mi.then_inc(st_cm, 1)

                def a_evacCQ(t):
                    ACT.wait_ge(st_cm, t + 1)
                    ACT.wait_ge(st_q, t + 1)
                    if t >= 1:
                        ACT.wait_ge(sv_q1, t)
                    ACT.copy(cqs_b[:, 0:D], ps_tr[:, :])
                    ACT.drain()
                    if t + 1 < NT:
                        ACT.dma_start(cry_b[0:1, ((t + 1) % 2) * D:((t + 1) % 2 + 1) * D],
                                      cqs_b[127:128, 0:D]).then_inc(s_cy2[(t + 1) % 2], 16)
                    ACT.copy(cqs_b[:, D:2 * D], ps_q[:, :]).then_inc(sc_cq, 1)

                def d_queried(t):
                    DVE.wait_ge(sc_cq, t + 1)
                    if t >= 1:
                        DVE.wait_ge(sc_nm, t)
                    cmul(qd_b, cqs_b[:, 0:D], cqs_b[:, D:2 * D])
                    DVE.engine_nop().then_inc(sv_q1, 1)

                def a_norm(t):
                    ACT.wait_ge(sv_q1, t + 1)
                    ACT.activation(sq_f[:, :], qd_b[:, :], ACTF.Square, accum_out=nrm[:, 0:1])
                    ACT.drain()
                    ACT.activation(nrm[:, 1:2], nrm[:, 0:1], ACTF.Ln)
                    ACT.drain()
                    ACT.activation(nrm[:, 2:3], nrm[:, 1:2], ACTF.Exp, scale=0.5)
                    ACT.drain()
                    ACT.nop().then_inc(sc_nm, 1)

                def d_qn(t):
                    DVE.wait_ge(sc_nm, t + 1)
                    DVE.tensor_scalar(nrm[:, 3:4], nrm[:, 2:3], 1e-9, None, ALU.add)
                    DVE.drain()
                    DVE.reciprocal(nrm[:, 3:4], nrm[:, 3:4])
                    DVE.drain()
                    if t >= 1:
                        DVE.wait_ge(st_qT, t)
                    DVE.tensor_scalar(qn_b[:, :], qd_b[:, :], nrm[:, 3:4], None, ALU.mult)
                    DVE.drain()
                    DVE.engine_nop().then_inc(sv_qn, 1)

                def p_trQ(t):
                    PE.wait_ge(sv_qn, t + 1)
                    PE.wait_ge(sc_cq, t + 1)
                    for cc in range(8):
                        mi = PE.matmul(ps_tr[:, cc * 128:(cc + 1) * 128],
                                       qn_b[:, cc * 128:(cc + 1) * 128], id_b[:, :],
                                       start=True, stop=True)
                    mi.then_inc(st_qT, 1)

                def a_evacQn(t):
                    ACT.wait_ge(st_qT, t + 1)
                    if t >= 1:
                        ACT.wait_ge(st_id, t)
                    ACT.copy(qnT_b[:, :], ps_tr[:, :]).then_inc(sc_qT, 1)

                def p_idft(t):
                    PE.wait_ge(sc_qT, t + 1)
                    if t == 0:
                        PE.wait_ge(s_ft, 112)
                    if t + 1 < NT:
                        PE.wait_ge(sc_v, t + 2)
                    else:
                        PE.wait_ge(sv_x1, t)
                    for fc in range(8):
                        for h in range(2):
                            mi = PE.matmul(ps_id[:, 512 * h:512 * (h + 1)],
                                           qnT_b[:, fc * 128:(fc + 1) * 128],
                                           gm_b[:, fc * D + 512 * h: fc * D + 512 * (h + 1)],
                                           start=(fc == 0), stop=(fc == 7))
                    mi.then_inc(st_id, 1)

                def d_x1(t):
                    g = NPT + t
                    DVE.wait_ge(st_id, t + 1)
                    if t >= 2:
                        DVE.wait_ge(ss[t % 2], 16 * (t // 2))
                    x1 = x1_sb[:, (t % 2) * D:(t % 2 + 1) * D]
                    DVE.tensor_add(x1, xsrc(g), ps_id[:, :])
                    DVE.drain()
                    DVE.engine_nop().then_inc(sv_x1, 1)
                    DVE.engine_nop().then_inc(sv_td, 1)

                SP.wait_ge(s_cst, CST)
                for g0 in range(min(3, GTOT)):
                    sp_load(g0)
                d_stats(0)
                a_lnexp(0)
                if GTOT > 1:
                    d_stats(1)
                    a_lnexp(1)
                for pt in range(NPT):
                    if pt + 3 < GTOT:
                        sp_load(pt + 3)
                    d_apply(pt)
                    if pt + 2 < GTOT:
                        d_stats(pt + 2)
                        a_lnexp(pt + 2)
                    p_xbar(pt)

                MV_TOT = [0]
                SCM_TOT = [0]
                nm_t, nm_a, nm_v = 0, 0, 0
                PE.wait_ge(sv_ln, NPT)
                for dc in range(8):
                    ps = ps_tr if (dc % 2 == 0) else ps_kv
                    if dc >= 2:
                        PE.wait_ge(sc_m, nm_a - 1)
                    for pt in range(NPT):
                        for h in range(2):
                            mi = PE.matmul(ps[:, 512 * h:512 * (h + 1)],
                                           xln_all[:, pt * D + dc * 128: pt * D + (dc + 1) * 128],
                                           xln_all[:, pt * D + 512 * h: pt * D + 512 * (h + 1)],
                                           start=(pt == 0), stop=(pt == NPT - 1))
                    mi.then_inc(st_m, 1)
                    nm_t += 1
                    ACT.wait_ge(st_m, nm_t)
                    ACT.copy(m_sb[:, dc * D:(dc + 1) * D], ps[:, :]).then_inc(sc_m, 1)
                    nm_a += 1
                ACT.wait_ge(st_cy, NPT)
                ACT.copy(xbr_row[0:1, :], ps_cy[0:1, :]).then_inc(sc_m, 1)
                nm_a += 1
                PE.wait_ge(sc_m, nm_a)
                for cc in range(8):
                    mi = PE.matmul(ps_q[:, cc:cc + 1], xbr_row[0:1, cc * 128:(cc + 1) * 128],
                                   orow_b[0:1, 0:1], start=True, stop=True)
                mi.then_inc(st_m, 1)
                nm_t += 1
                ACT.wait_ge(st_m, nm_t)
                ACT.copy(xbr_col[:, 0:8], ps_q[:, 0:8]).then_inc(sc_m, 1)
                nm_a += 1
                for wi, wt_b in enumerate((wtk_b, wtv_b)):
                    PE.wait_ge(sc_m, nm_a)
                    for cc in range(8):
                        for h in range(2):
                            mi = PE.matmul(ps_cy[0:1, 512 * h:512 * (h + 1)],
                                           xbr_col[:, cc:cc + 1],
                                           wt_b[:, cc * D + 512 * h: cc * D + 512 * (h + 1)],
                                           start=(cc == 0), stop=(cc == 7))
                    mi.then_inc(st_m, 1)
                    nm_t += 1
                    ACT.wait_ge(st_m, nm_t)
                    ACT.copy(axcx[0:1, wi * D:(wi + 1) * D], ps_cy[0:1, :]).then_inc(sc_m, 1)
                    nm_a += 1
                for dc in range(8):
                    psG = ps_tr if (dc % 2 == 0) else ps_kv
                    if dc >= 2:
                        PE.wait_ge(sv_m, nm_v - 1)
                    for c8 in range(8):
                        for h in range(2):
                            mi = PE.matmul(psG[:, 512 * h:512 * (h + 1)],
                                           m_sb[:, c8 * D + dc * 128: c8 * D + (dc + 1) * 128],
                                           wtv_b[:, c8 * D + 512 * h: c8 * D + 512 * (h + 1)],
                                           start=(c8 == 0), stop=(c8 == 7))
                    mi.then_inc(st_m, 1)
                    nm_t += 1
                    DVE.wait_ge(st_m, nm_t)
                    DVE.tensor_mul(hh[:, 0:512], wtk_b[:, dc * D:dc * D + 512], psG[:, 0:512])
                    DVE.tensor_mul(hh[:, 512:1024], wtk_b[:, dc * D + 512:dc * D + 1024],
                                   psG[:, 512:1024])
                    DVE.tensor_mul(hh[:, 1024:1536], wtk_b[:, dc * D:dc * D + 512],
                                   psG[:, 512:1024])
                    DVE.tensor_mul(hh[:, 1536:2048], wtk_b[:, dc * D + 512:dc * D + 1024],
                                   psG[:, 0:512])
                    DVE.drain()
                    DVE.engine_nop().then_inc(sv_m, 1)
                    nm_v += 1
                    PE.wait_ge(sv_m, nm_v)
                    if dc == 0:
                        PE.wait_ge(sc_m, nm_a)
                    PE.matmul(ps_cy[0:1, 0:512], ocol_b[:, :], hh[:, 0:512],
                              start=(dc == 0), stop=(dc == 7))
                    PE.matmul(ps_cy[0:1, 512:1024], ocol_b[:, :], hh[:, 512:1024],
                              start=(dc == 0), stop=(dc == 7))
                    PE.matmul(ps_q[0:1, 0:512], ocol_b[:, :], hh[:, 1024:1536],
                              start=(dc == 0), stop=(dc == 7))
                    mi = PE.matmul(ps_q[0:1, 512:1024], ocol_b[:, :], hh[:, 1536:2048],
                                   start=(dc == 0), stop=(dc == 7))
                    if dc == 7:
                        mi.then_inc(st_m, 1)
                        nm_t += 1
                ACT.wait_ge(st_m, nm_t)
                ACT.copy(s_sb[0:1, 0:1024], ps_cy[0:1, :])
                ACT.drain()
                ACT.copy(s_sb[0:1, 1024:2048], ps_q[0:1, 0:1024]).then_inc(sc_m, 1)
                nm_a += 1
                DVE.wait_ge(sc_m, nm_a)
                for j, (A, B) in enumerate(((0, 0), (1, 1), (0, 1), (1, 0))):
                    bk = biasP[0:1, A * 512:(A + 1) * 512]
                    bv = biasP[0:1, D + B * 512:D + (B + 1) * 512]
                    cx = axcx[0:1, D + B * 512:D + (B + 1) * 512]
                    ax = axcx[0:1, A * 512:(A + 1) * 512]
                    t0 = tr1[0:1, 0:512]
                    DVE.tensor_mul(t0, bk, cx)
                    DVE.drain()
                    DVE.tensor_add(s_sb[0:1, j * 512:(j + 1) * 512],
                                   s_sb[0:1, j * 512:(j + 1) * 512], t0)
                    DVE.drain()
                    DVE.tensor_mul(t0, bv, ax)
                    DVE.drain()
                    DVE.tensor_add(s_sb[0:1, j * 512:(j + 1) * 512],
                                   s_sb[0:1, j * 512:(j + 1) * 512], t0)
                    DVE.drain()
                    DVE.tensor_mul(t0, bk, bv)
                    DVE.drain()
                    DVE.scalar_tensor_tensor(s_sb[0:1, j * 512:(j + 1) * 512], t0,
                                             float(NPT * 128),
                                             s_sb[0:1, j * 512:(j + 1) * 512],
                                             ALU.mult, ALU.add)
                    DVE.drain()
                DVE.tensor_sub(tr1[0:1, 1:512], s_sb[0:1, 1:512], s_sb[0:1, 513:1024])
                DVE.tensor_copy(tr1[0:1, 0:1], s_sb[0:1, 0:1])
                DVE.tensor_copy(tr1[0:1, 512:513], s_sb[0:1, 512:513])
                DVE.tensor_add(tr1[0:1, 513:1024], s_sb[0:1, 1025:1536], s_sb[0:1, 1537:2048])
                DVE.drain()
                DVE.tensor_scalar(cry_b[0:1, 0:D], tr1[0:1, 0:1024],
                                  mask_f[0:1, 0:1], None, ALU.mult)
                DVE.drain()
                DVE.engine_nop().then_inc(sv_cr, 1)
                MV_TOT[0] = nm_v
                SCM_TOT[0] = nm_a
            ps_id = stA.enter_context(nc.psum_tensor("ps_id", [128, D], F32))
            gm_b = sbA("gm_b", [128, 8 * D], BF16)
            PL.wait_ge(sc_f, FOLD)
            PL.dma_start(gm_b[:, :], gm_d[:, :]).then_inc(s_ft, 16)
            PL.wait_ge(sv_cr, 1)
            for wi, wsrc in enumerate((wia_d, wib_d, w1a_d, w1b_d)):
                PL.dma_start(wabs_d.ap()[wi], wsrc.ap()).then_inc(s_cvt, 16)
            PL.dma_start(w2s_d.ap()[0], wo_d.ap()).then_inc(s_cvt, 16)
            for e in range(NE):
                PL.dma_start(w2s_d.ap()[1 + e], w2_d.ap()[e]).then_inc(s_cvt, 16)
            qd_b = sbA("qd_b", [128, D], BF16)
            qn_b = sbA("qn_b", [128, D], BF16)
            qnT_b = sbA("qnT_b", [128, D], BF16)
            x1_sb = sbA("x1_sb", [128, 2 * D], F32)
            nrm = sbA("nrm", [128, 4], F32)
            prod = sbA("prod", [128, 4 * 512], BF16)
            kv_b = sbA("kv_b", [128, 2 * D], BF16)
            kvs_b = sbA("kvs_b", [128, 2 * D], BF16)
            cqs_b = sbA("cqs_b", [128, 2 * D], BF16)
            sq_f = sbA("sq_f", [128, D], F32)

            d_apply(NPT)
            if NT > 1:
                d_apply(NPT + 1)
            p_trX(0)
            a_evacX(0)
            p_projK(0)
            a_evacK(0)
            p_projQ(0)
            p_projV(0)
            a_evacV(0)
            d_products(0)
            for t in range(NT):
                g = NPT + t
                if g + 3 < GTOT:
                    sp_load(g + 3)
                p_cum(t)
                if t + 1 < NT:
                    p_trX(t + 1)
                if t >= 1:
                    p_idft(t - 1)
                if g + 2 < GTOT:
                    ln_chain(g + 2)
                if t >= 1:
                    d_x1(t - 1)
                d_queried(t)
                if t + 1 < NT:
                    a_evacX(t + 1)
                a_evacCQ(t)
                if t + 1 < NT:
                    p_projK(t + 1)
                    a_evacK(t + 1)
                a_norm(t)
                d_qn(t)
                if t + 1 < NT:
                    p_projV(t + 1)
                    a_evacV(t + 1)
                p_trQ(t)
                if t + 1 < NT:
                    p_projQ(t + 1)
                a_evacQn(t)
                if t + 1 < NT:
                    d_products(t + 1)
                if t >= 1:
                    SP.wait_ge(sv_x1, t)
                    SP.dma_start(x1s_t[t - 1, :, :],
                                 x1_sb[:, ((t - 1) % 2) * D:((t - 1) % 2 + 1) * D]
                                 ).then_inc(ss[(t - 1) % 2], 16)
            p_idft(NT - 1)
            d_x1(NT - 1)
            SP.wait_ge(sv_x1, NT)
            SP.dma_start(x1s_t[NT - 1, :, :],
                         x1_sb[:, ((NT - 1) % 2) * D:((NT - 1) % 2 + 1) * D]
                         ).then_inc(ss[(NT - 1) % 2], 16)

        s_ac = sem("s_ac")
        s_out = sem("s_out")
        s_wg = sem("s_wg")
        s_wa = [sem(f"s_wa{i}") for i in range(2)]
        s_wb = [sem(f"s_wb{i}") for i in range(2)]
        s_x1t = [sem(f"s_x1t{i}") for i in range(2)]
        s_w2 = [sem(f"s_w2{i}") for i in range(2)]
        st_tb = sem("st_tb")
        sc_ev = sem("sc_ev")
        st_g = sem("st_g")
        sc_gx = sem("sc_gx")
        sv_g = sem("sv_g")
        sv_se = sem("sv_se")
        sc_gl = sem("sc_gl")
        st_ab = sem("st_ab")
        sc_sil = sem("sc_sil")
        sv_sT = sem("sv_sT")
        st_r = sem("st_r")
        sv_acc = sem("sv_acc")
        sv_fin = sem("sv_fin")

        SS0_TOT = 16 * ((NT + 1) // 2)
        SS1_TOT = 16 * (NT // 2)

        with ExitStack() as stB:
            sbB = lambda name, shape, dt: stB.enter_context(nc.sbuf_tensor(name, shape, dt))
            acc = sbB("acc", [128, TPC * D], F32)
            hiT = sbB("hiT", [128, 8 * CH], BF16)
            hsT = sbB("hsT", [128, 16 * CH], BF16)
            w2_b = [sbB(f"w2_{i}", [128, 16 * D], BF16) for i in range(2)]
            wab_b = [sbB(f"wab_{i}", [128, 2 * D], BF16) for i in range(2)]
            wg_hi = sbB("wg_hi", [128, 80], BF16)
            wg_f = sbB("wg_f", [128, 80], F32)
            gbeb_f = sbB("gbeb_f", [1, 10], F32)
            gbeb_b = sbB("gbeb_b", [1, 20], BF16)
            wse = sbB("wse", [128, NE * TPC], F32)
            hi_b = sbB("hi_b", [128, TPC * D], BF16)
            x1t = sbB("x1t", [128, 2 * D], F32)
            sil_b = sbB("sil_b", [128, 2 * D], BF16)
            st2 = sbB("st2", [128, 12], F32)
            ag2 = sbB("ag2", [128, 2 * TPC], F32)
            rc2 = sbB("rc2", [128, 2 * TPC], F32)
            gc = sbB("gc", [128, 32], F32)
            gb8 = sbB("gb8", [128, 96], F32)
            ge2 = sbB("ge2", [128, 4], F32)
            glog = sbB("glog", [128, 80], F32)

            ps_tb = stB.enter_context(nc.psum_tensor("ps_tb", [128, D], F32))
            ps_a = stB.enter_context(nc.psum_tensor("ps_a", [128, D], F32))
            ps_b = stB.enter_context(nc.psum_tensor("ps_b", [128, D], F32))
            ps_r = stB.enter_context(nc.psum_tensor("ps_r", [128, D], F32))

            PL.wait_ge(sv_x1, NT)
            PL.wait_ge(st_id, NT)
            PL.wait_ge(sc_qT, NT)
            PL.wait_ge(ss[0], SS0_TOT)
            PL.wait_ge(ss[1], SS1_TOT)
            PL.wait_ge(s_cy2[0], 16 * ((NT - 2) // 2))
            PL.wait_ge(s_cy2[1], 16 * (NT // 2))
            PL.dma_start(wg_f[:, :], wg_d[:, :]).then_inc(s_wg, 16)
            PL.dma_start(gbeb_f[:, :], gb_d[:, :]).then_inc(s_wg, 16)
            DVE.wait_ge(s_wg, 32)
            DVE.tensor_copy(wg_hi[:, :], wg_f[:, :])
            DVE.drain()
            DVE.tensor_copy(gbeb_b[0:1, 0:10], gbeb_f[0:1, :])
            DVE.drain()
            DVE.tensor_sub(gbeb_b[0:1, 10:20], gbeb_f[0:1, :], gbeb_b[0:1, 0:10])
            DVE.drain()
            DVE.engine_nop().then_inc(sv_g, 1)

            SP.wait_ge(sv_x1, NT)
            SP.wait_ge(st_id, NT)
            SP.wait_ge(sc_qT, NT)
            SP.wait_ge(ss[0], SS0_TOT)
            SP.wait_ge(ss[1], SS1_TOT)
            SP.wait_ge(s_cy2[0], 16 * ((NT - 2) // 2))
            SP.wait_ge(s_cy2[1], 16 * (NT // 2))

            out_r = out_d.ap().rearrange("(ch tl p) d -> ch p tl d", p=128, tl=TPC)
            out_t4 = out_d.ap().rearrange("(ch tl p) d -> ch tl p d", p=128, tl=TPC)
            x1r = x1s_d.ap().rearrange("(ch tl p) d -> ch p tl d", p=128, tl=TPC)
            acc3 = acc.ap().rearrange("p (tl d) -> p tl d", d=D)

            RPC = TPC + NE * TPC
            ctr = {"g": 1, "tb": 0, "gx": 0}
            gx_done = {}
            gx_ln = {}
            ev_tile = {}
            g_hl = {}
            n_ab = 0
            n_r = 0
            for c in range(NCH):
                def s1_chain(cc2, i):
                    G = cc2 * TPC + i
                    r = G % 2
                    src2 = x1t[:, r * D:(r + 1) * D]
                    if G >= 2:
                        SP.wait_ge(sv_g, g_hl[G - 2])
                    SP.dma_start(src2, x1s_d.ap().rearrange("(n p) d -> n p d", p=128)[G]
                                 ).then_inc(s_x1t[r], 16)
                    DVE.wait_ge(s_x1t[r], 16 * (G // 2 + 1))
                    if G >= TPC:
                        DVE.wait_ge(st_tb, G - TPC + 1)
                    DVE.bn_stats(st2[:, 0:6], src2[:, 0:512])
                    DVE.bn_stats(st2[:, 6:12], src2[:, 512:1024])
                    DVE.drain()
                    DVE.bn_aggr(ag2[:, 2 * i:2 * i + 2], st2[:, :])
                    DVE.drain()
                    DVE.engine_nop().then_inc(sv_g, 1)
                    ctr["g"] += 1
                    ACT.wait_ge(sv_g, ctr["g"])
                    ACT.activation(rc2[:, 2 * i + 1:2 * i + 2], ag2[:, 2 * i + 1:2 * i + 2],
                                   ACTF.Ln, bias=eps_c[:, 0:1])
                    ACT.drain()
                    ACT.activation(rc2[:, 2 * i:2 * i + 1], rc2[:, 2 * i + 1:2 * i + 2],
                                   ACTF.Exp, scale=-0.5)
                    ACT.drain()
                    ACT.nop().then_inc(sc_gx, 1)
                    ctr["gx"] += 1
                    gx_ln[G] = ctr["gx"]
                    DVE.wait_ge(sc_gx, ctr["gx"])
                    DVE.tensor_scalar(hi_b[:, i * D:(i + 1) * D], src2, ag2[:, 2 * i:2 * i + 1],
                                      rc2[:, 2 * i:2 * i + 1], ALU.subtract, ALU.mult)
                    DVE.drain()
                    DVE.engine_nop().then_inc(sv_g, 1)
                    ctr["g"] += 1
                    g_hl[G] = ctr["g"]

                def s1_pe(i):
                    G = c * TPC + i
                    q = i % 2
                    tr_ps = ps_tb if (i % 2 == 0) else ps_a
                    PE.wait_ge(sv_g, g_hl[G])
                    if c > 0 and i == 0:
                        PE.wait_ge(sv_acc, c * RPC)
                        PE.wait_ge(sv_sT, 32 * c)
                        PE.wait_ge(sc_sil, 32 * c)
                    if ctr["tb"] >= 2:
                        PE.wait_ge(sc_ev, ctr["tb"] - 1)
                    for dc in range(8):
                        mi = PE.matmul(tr_ps[:, dc * 128:(dc + 1) * 128],
                                       hi_b[:, i * D + dc * 128: i * D + (dc + 1) * 128],
                                       id_b[:, :], start=True, stop=True)
                    mi.then_inc(st_tb, 1)
                    ctr["tb"] += 1
                    DVE.wait_ge(st_tb, ctr["tb"])
                    if i == 0:
                        DVE.wait_ge(st_ab, 32 * c)
                        DVE.wait_ge(st_g, c * TPC)
                    DVE.tensor_copy(
                        hiT.ap().rearrange("p (dc n) -> p dc n", dc=8)[:, :, i * 128:(i + 1) * 128],
                        tr_ps.ap().rearrange("p (dc n) -> p dc n", dc=8),
                    )
                    DVE.drain()
                    DVE.engine_nop().then_inc(sc_ev, 1)
                    ev_tile[G] = ctr["tb"]
                def s1_gate_mm(i):
                    G = c * TPC + i
                    w = ps_b[:, 10 * i:10 * i + 10]
                    PE.wait_ge(sc_ev, ev_tile[G])
                    if G == 0:
                        PE.wait_ge(s_wg, 32)
                        PE.wait_ge(sv_g, 1)
                    for dc in range(8):
                        PE.matmul(w,
                                  hiT[:, dc * CH + i * 128: dc * CH + (i + 1) * 128],
                                  wg_hi[:, dc * 10:(dc + 1) * 10],
                                  start=(dc == 0), stop=False)
                    PE.matmul(w, orow_b[0:1, :], gbeb_b[0:1, 0:10],
                              start=False, stop=False)
                    PE.matmul(w, orow_b[0:1, :], gbeb_b[0:1, 10:20],
                              start=False, stop=True).then_inc(st_g, 1)

                def s1_gate(i):
                    G = c * TPC + i
                    q = i % 2
                    gl = glog[:, 10 * i:10 * i + 10]
                    DVE.wait_ge(sc_gl, c + 1)
                    el = gl[:, 2:10]
                    gcq = gc[:, 16 * q:16 * q + 16]
                    gbq = gb8[:, 48 * q:48 * q + 48]
                    msk, eex = gbq[:, 0:8], gbq[:, 8:16]
                    p8, t8 = gbq[:, 16:24], gbq[:, 24:32]
                    p2, k8 = gbq[:, 32:40], gbq[:, 40:48]
                    DVE.tensor_tensor(gcq[:, 0:1], gl[:, 1:2], gl[:, 0:1], ALU.is_gt)
                    DVE.reduce_max(gcq[:, 1:2], gl[:, 0:2], axis=AX.X)
                    DVE.drain()
                    DVE.tensor_scalar(gcq[:, 1:2], gcq[:, 1:2], -1.0, None, ALU.mult)
                    DVE.tensor_scalar(gcq[:, 4:5], gcq[:, 0:1], -BIG, None, ALU.mult)
                    DVE.tensor_scalar(gcq[:, 5:6], gcq[:, 0:1], 1.0, BIG, ALU.subtract, ALU.mult)
                    DVE.drain()
                    DVE.tensor_scalar(msk[:, 0:4], el[:, 0:4], gcq[:, 4:5], None, ALU.add)
                    DVE.tensor_scalar(msk[:, 4:8], el[:, 4:8], gcq[:, 5:6], None, ALU.add)
                    DVE.drain()
                    DVE.reduce_max(gcq[:, 6:7], msk[:, :], axis=AX.X)
                    DVE.drain()
                    DVE.tensor_scalar(gcq[:, 6:7], gcq[:, 6:7], -1.0, None, ALU.mult)
                    DVE.drain()
                    DVE.engine_nop().then_inc(sv_g, 1)
                    ctr["g"] += 1
                    ACT.wait_ge(sv_g, ctr["g"])
                    if G >= 1:
                        ACT.wait_ge(sv_se, G)
                    ACT.activation(ge2[:, 2 * q:2 * q + 2], gl[:, 0:2], ACTF.Exp,
                                   bias=gcq[:, 1:2])
                    ACT.drain()
                    ACT.activation(eex[:, :], msk[:, :], ACTF.Exp, bias=gcq[:, 6:7])
                    ACT.drain()
                    ACT.nop().then_inc(sc_gx, 2)
                    ctr["gx"] += 2
                    DVE.wait_ge(sc_gx, ctr["gx"])
                    DVE.reduce_sum(gcq[:, 2:3], ge2[:, 2 * q:2 * q + 2], axis=AX.X)
                    DVE.reduce_sum(gcq[:, 7:8], eex[:, :], axis=AX.X)
                    DVE.drain()
                    DVE.reciprocal(gcq[:, 3:4], gcq[:, 2:3])
                    DVE.reciprocal(gcq[:, 8:9], gcq[:, 7:8])
                    DVE.drain()
                    DVE.tensor_mul(gcq[:, 9:10], gcq[:, 8:9], gcq[:, 3:4])
                    DVE.drain()
                    DVE.tensor_scalar(p8[:, :], eex[:, :], gcq[:, 9:10], None, ALU.mult)
                    DVE.drain()
                    DVE.reduce_max(gcq[:, 10:11], p8[:, :], axis=AX.X)
                    DVE.drain()
                    DVE.tensor_scalar(t8[:, :], p8[:, :], gcq[:, 10:11], None, ALU.is_ge)
                    DVE.drain()
                    DVE.scalar_tensor_tensor(p2[:, :], t8[:, :], -BIG, p8[:, :],
                                             ALU.mult, ALU.add)
                    DVE.drain()
                    DVE.reduce_max(gcq[:, 11:12], p2[:, :], axis=AX.X)
                    DVE.drain()
                    DVE.tensor_scalar(k8[:, :], p8[:, :], gcq[:, 11:12], None, ALU.is_ge)
                    DVE.drain()
                    DVE.tensor_mul(wse[:, i * NE:(i + 1) * NE], k8[:, :], p8[:, :])
                    DVE.drain()
                    DVE.engine_nop().then_inc(sv_se, 1)

                def wout_load(cc):
                    kw2 = cc * 9
                    bw2 = kw2 % 2
                    if kw2 == 0:
                        PL.wait_ge(s_cvt, CVT)
                    if kw2 >= 2:
                        a2, j2 = divmod(kw2 - 2, 9)
                        PL.wait_ge(st_r, a2 * RPC + (j2 + 1) * TPC)
                    for h2 in range(2):
                        PL.dma_start(w2_b[bw2][:, h2 * 8 * D:(h2 + 1) * 8 * D],
                                     w2s_d.ap()[0, :, h2 * 8 * D:(h2 + 1) * 8 * D]
                                     ).then_inc(s_w2[bw2], 16)

                if c == 0:
                    for i in range(TPC):
                        s1_chain(0, i)
                for i in range(TPC):
                    s1_pe(i)
                for i in range(TPC):
                    s1_gate_mm(i)
                DVE.wait_ge(st_g, TPC * (c + 1))
                if c > 0:
                    DVE.wait_ge(sc_gx, gx_done[c - 1])
                DVE.tensor_copy(glog[:, :], ps_b[:, 0:10 * TPC])
                DVE.drain()
                DVE.engine_nop().then_inc(sc_gl, 1)

                for kind in range(2):
                    wa_d, wb_d2 = (wabs_d.ap()[2 * kind], wabs_d.ap()[2 * kind + 1])
                    for hc in range(16):
                        gidx = (c * 2 + kind) * 16 + hc
                        bi = gidx % 2
                        pa, pb = (ps_a, ps_b) if (gidx % 2 == 0) else (ps_tb, ps_r)
                        sil = sil_b[:, (gidx % 2) * D:(gidx % 2) * D + CH]
                        if gidx == 0:
                            SP.wait_ge(s_cvt, CVT)
                            PL.wait_ge(s_cvt, CVT)
                        if gidx >= 2:
                            SP.wait_ge(sv_sT, gidx - 1)
                            PL.wait_ge(sv_sT, gidx - 1)
                        SP.dma_start(wab_b[bi][:, 0:D], wa_d[hc]).then_inc(s_wa[bi], 16)
                        PL.dma_start(wab_b[bi][:, D:2 * D], wb_d2[hc]).then_inc(s_wb[bi], 16)
                        PE.wait_ge(s_wa[bi], 16 * (gidx // 2 + 1))
                        if gidx >= 2:
                            PE.wait_ge(sv_sT, gidx - 1)
                        if kind == 0 and hc == 0:
                            PE.wait_ge(sc_gl, c + 1)
                        if kind == 0 and hc <= 1:
                            PE.wait_ge(sc_ev, TPC * (c + 1))
                        if kind == 1 and hc <= 1:
                            PE.wait_ge(sv_acc, c * RPC + TPC)
                        WL = min(512, CH)
                        for dc in range(8):
                            for h in range(CH // WL):
                                PE.matmul(pa[:, WL * h:WL * (h + 1)],
                                          wab_b[bi][:, dc * 128:(dc + 1) * 128],
                                          hiT[:, dc * CH + WL * h: dc * CH + WL * (h + 1)],
                                          start=(dc == 0), stop=(dc == 7))
                        PE.wait_ge(s_wb[bi], 16 * (gidx // 2 + 1))
                        for dc in range(8):
                            for h in range(CH // WL):
                                mi = PE.matmul(pb[:, WL * h:WL * (h + 1)],
                                               wab_b[bi][:, D + dc * 128: D + (dc + 1) * 128],
                                               hiT[:, dc * CH + WL * h: dc * CH + WL * (h + 1)],
                                               start=(dc == 0), stop=(dc == 7))
                        mi.then_inc(st_ab, 1)
                        n_ab += 1
                        ACT.wait_ge(st_ab, n_ab)
                        ACT.activation(sil, pa[:, 0:CH], ACTF.Silu)
                        ACT.drain()
                        ACT.nop().then_inc(sc_sil, 1)
                        DVE.wait_ge(sc_sil, n_ab)
                        if hc == 0:
                            if kind == 0 and c > 0:
                                DVE.wait_ge(st_r, c * RPC)
                            if kind == 1:
                                DVE.wait_ge(st_r, c * RPC + TPC)
                        DVE.tensor_mul(hsT[:, hc * CH:(hc + 1) * CH], sil, pb[:, 0:CH])
                        DVE.drain()
                        DVE.engine_nop().then_inc(sv_sT, 1)
                    if kind == 0:
                        for i2 in range(TPC):
                            s1_gate(i2)
                        gx_done[c] = ctr["gx"]
                        if c > 0:
                            SP.wait_ge(s_out, 128 * c)
                            SP.wait_ge(sv_fin, c)
                        SP.dma_start(acc3, x1r[c], ).then_inc(s_ac, 16)
                        kw = c * 9
                        bw = kw % 2
                        if c == 0:
                            wout_load(0)
                        for i in range(TPC):
                            pr = ps_r if (n_r % 2 == 0) else ps_tb
                            PE.wait_ge(sv_sT, c * 32 + 16)
                            PE.wait_ge(s_w2[bw], 32 * (kw // 2 + 1))
                            if n_r >= 2:
                                PE.wait_ge(sv_acc, n_r - 1)
                            for hc in range(16):
                                for h in range(2):
                                    mi = PE.matmul(pr[:, 512 * h:512 * (h + 1)],
                                                   hsT[:, hc * CH + i * 128: hc * CH + (i + 1) * 128],
                                                   w2_b[bw][:, hc * D + 512 * h: hc * D + 512 * (h + 1)],
                                                   start=(hc == 0), stop=(hc == 15))
                            mi.then_inc(st_r, 1)
                            n_r += 1
                            DVE.wait_ge(st_r, n_r)
                            DVE.wait_ge(s_ac, 16 * (c + 1))
                            DVE.tensor_add(acc[:, i * D:(i + 1) * D],
                                           acc[:, i * D:(i + 1) * D], pr[:, :])
                            DVE.drain()
                            DVE.engine_nop().then_inc(sv_acc, 1)
                for e in range(NE):
                    if c + 1 < NCH and e < TPC:
                        s1_chain(c + 1, e)
                    gw = c * 9 + 1 + e
                    bi = gw % 2
                    if gw >= 2:
                        cc, jj = divmod(gw - 2, 9)
                        PL.wait_ge(st_r, cc * RPC + (jj + 1) * TPC)
                    for h2 in range(2):
                        PL.dma_start(w2_b[bi][:, h2 * 8 * D:(h2 + 1) * 8 * D],
                                     w2s_d.ap()[1 + e, :, h2 * 8 * D:(h2 + 1) * 8 * D]
                                     ).then_inc(s_w2[bi], 16)
                    for i in range(TPC):
                        pr = ps_r if (n_r % 2 == 0) else ps_tb
                        PE.wait_ge(s_w2[bi], 32 * (gw // 2 + 1))
                        if n_r >= 2:
                            PE.wait_ge(sv_acc, n_r - 1)
                        if e == 0 and i == 0:
                            PE.wait_ge(sv_sT, c * 32 + 32)
                        for hc in range(16):
                            for h in range(2):
                                mi = PE.matmul(pr[:, 512 * h:512 * (h + 1)],
                                               hsT[:, hc * CH + i * 128: hc * CH + (i + 1) * 128],
                                               w2_b[bi][:, hc * D + 512 * h: hc * D + 512 * (h + 1)],
                                               start=(hc == 0), stop=(hc == 15))
                        mi.then_inc(st_r, 1)
                        n_r += 1
                        DVE.wait_ge(st_r, n_r)
                        DVE.wait_ge(sv_se, c * TPC + i + 1)
                        DVE.scalar_tensor_tensor(acc[:, i * D:(i + 1) * D], pr[:, :],
                                                 wse[:, i * NE + e: i * NE + e + 1],
                                                 acc[:, i * D:(i + 1) * D],
                                                 ALU.mult, ALU.add)
                        DVE.drain()
                        DVE.engine_nop().then_inc(sv_acc, 1)
                        if e == NE - 1:
                            ACT.wait_ge(sv_acc, n_r)
                            ACT.dma_start(out_t4[c, i], acc[:, i * D:(i + 1) * D]
                                          ).then_inc(s_out, 16)
                if c + 1 < NCH:
                    wout_load(c + 1)
                DVE.engine_nop().then_inc(sv_fin, 1)
            SP.wait_ge(s_out, 128 * NCH)
    return nc




def kernel(L0_attn_x, L0_attn_Wk, L0_attn_bk, L0_attn_Wv, L0_attn_bv,
           L0_attn_Wq, L0_attn_bq, L0_ffn_w_in, L0_ffn_w_out, L0_ffn_w1,
           L0_ffn_w2, L0_ffn_wg, L0_ffn_we, L0_ffn_gb, L0_ffn_eb):
    Bsz, S, _ = L0_attn_x.shape
    T = S // 2
    xf = np.ascontiguousarray(np.asarray(L0_attn_x, dtype=np.float32)).reshape(Bsz * S, D)
    consts = _make_consts()

    def colmaj(v):
        return np.ascontiguousarray(np.asarray(v, dtype=np.float32).reshape(8, 128).T)

    w_in = np.asarray(L0_ffn_w_in, dtype=np.float32)
    w1 = np.asarray(L0_ffn_w1, dtype=np.float32)
    w_out = np.asarray(L0_ffn_w_out, dtype=np.float32)
    w2 = np.asarray(L0_ffn_w2, dtype=np.float32)
    wg = np.asarray(L0_ffn_wg, dtype=np.float32)
    we = np.asarray(L0_ffn_we, dtype=np.float32)
    wgT = np.concatenate([wg, we], axis=0).T

    shared_w = {
        "Wk": _chunkmaj(np.asarray(L0_attn_Wk, dtype=np.float32)),
        "Wv": _chunkmaj(np.asarray(L0_attn_Wv, dtype=np.float32)),
        "Wq": _chunkmaj(np.asarray(L0_attn_Wq, dtype=np.float32)),
        "bk": colmaj(L0_attn_bk), "bv": colmaj(L0_attn_bv), "bq": colmaj(L0_attn_bq),
        "w_inT_a": _pieces_16(w_in[0:HID]),
        "w_inT_b": _pieces_16(w_in[HID:2 * HID]),
        "w1T_a": _pieces_16(w1[0:HID]),
        "w1T_b": _pieces_16(w1[HID:2 * HID]),
        "w_outT": _chunkmaj(np.ascontiguousarray(w_out.T)),
        "w2T": np.ascontiguousarray(np.stack(
            [_chunkmaj(np.ascontiguousarray(w2[e].T)) for e in range(NE)])),
        "wgT": _chunkmaj(np.ascontiguousarray(wgT)),
        "gbeb": np.ascontiguousarray(np.concatenate(
            [np.asarray(L0_ffn_gb, np.float32),
             np.asarray(L0_ffn_eb, np.float32)]).reshape(1, 10)),
        **consts,
    }

    nc = build_nc(T=T, TP=T, CH=1024)
    in_maps = []
    for c in range(NCORES):
        bidx, half = c // 2, c % 2
        base = bidx * S
        m = dict(shared_w)
        m["x"] = np.ascontiguousarray(xf[base + half * T: base + (half + 1) * T])
        m["xp"] = np.ascontiguousarray(xf[base: base + T])
        m["mask"] = np.full((1, 1), float(half), dtype=np.float32)
        in_maps.append(m)
    res = run_bass_kernel_spmd(nc, in_maps, core_ids=list(range(NCORES)))
    out = np.empty((Bsz * S, D), dtype=np.float32)
    for c in range(NCORES):
        bidx, half = c // 2, c % 2
        base = bidx * S
        out[base + half * T: base + (half + 1) * T] = res.results[c]["out"]
    return out.reshape(Bsz, S, D)


# version 31 (final)
# speedup vs baseline: 1.0312x; 1.0125x over previous
import math
from contextlib import ExitStack

import numpy as np
import ml_dtypes

import concourse.bass as bass
import concourse.mybir as mybir
from concourse.bass_utils import run_bass_kernel_spmd

F32 = mybir.dt.float32
BF16 = mybir.dt.bfloat16
ALU = mybir.AluOpType
ACTF = mybir.ActivationFunctionType
AX = mybir.AxisListType
BF = ml_dtypes.bfloat16

D = 1024
HID = 2048
NE = 8
NCORES = 8
BIG = 1e30



def _chunkmaj(a, p=128):
    R, C = a.shape
    return np.ascontiguousarray(
        a.reshape(R // p, p, C).transpose(1, 0, 2).reshape(p, (R // p) * C)
    )


def _make_consts():
    N = D
    j = np.arange(N, dtype=np.float64)
    f = np.arange(512, dtype=np.float64)
    ang = 2.0 * np.pi * np.outer(j, f) / N
    cosm = np.cos(ang)
    sinm = np.sin(ang)
    alt = np.where(np.arange(N) % 2 == 0, 1.0, -1.0)

    ft = np.zeros((N, N), dtype=np.float64)
    ft[:, 0:512] = cosm
    ft[:, 512] = alt
    ft[:, 513:1024] = -sinm[:, 1:512]
    s = np.ones((N,), dtype=np.float64)
    s[1:512] = np.sqrt(2.0)
    s[513:1024] = np.sqrt(2.0)
    ft_k = ft * s[None, :]
    ft_vq = ft
    ft_q = ft.copy()
    ft_q[:, 513:1024] = -ft_q[:, 513:1024]

    n_ = np.arange(N, dtype=np.float64)
    angn = 2.0 * np.pi * np.outer(f, n_) / N
    g = np.zeros((N, N), dtype=np.float64)
    c = 1.0 / (N * math.sqrt(D))
    g[0, :] = c
    g[1:512, :] = math.sqrt(2.0) * np.cos(angn[1:512, :]) * c
    g[512, :] = alt * c
    g[513:1024, :] = -math.sqrt(2.0) * np.sin(angn[1:512, :]) * c

    return {
        "ftk": _chunkmaj(ft_k.astype(np.float32)),
        "ftvq": _chunkmaj(ft_vq.astype(np.float32)),
        "ftq": _chunkmaj(ft_q.astype(np.float32)),
        "gmat": _chunkmaj(g.astype(np.float32)),
        "ident": np.eye(128, dtype=np.float32),
        "ltri": np.triu(np.ones((128, 128), dtype=np.float32)),
        "onesrow": np.ones((1, 128), dtype=np.float32),
        "onescol": np.ones((128, 1), dtype=np.float32),
    }


def _pieces_16(w):
    a = w.reshape(16, 128, 8, 128)
    return np.ascontiguousarray(a.transpose(0, 3, 2, 1).reshape(16, 128, 1024))




def build_nc(T=4096, TP=4096, CH=1024):
    assert T % CH == 0 and CH % 128 == 0 and TP % 128 == 0
    NT = T // 128
    NPT = TP // 128
    NCH = T // CH
    TPC = CH // 128

    nc = bass.Bass("TRN2", target_bir_lowering=False)

    x_d = nc.dram_tensor("x", [T, D], F32, kind="ExternalInput")
    xp_d = nc.dram_tensor("xp", [TP, D], F32, kind="ExternalInput")
    mask_d = nc.dram_tensor("mask", [1, 1], F32, kind="ExternalInput")
    wk_d = nc.dram_tensor("Wk", [128, 8 * D], F32, kind="ExternalInput")
    wv_d = nc.dram_tensor("Wv", [128, 8 * D], F32, kind="ExternalInput")
    wq_d = nc.dram_tensor("Wq", [128, 8 * D], F32, kind="ExternalInput")
    bk_d = nc.dram_tensor("bk", [128, 8], F32, kind="ExternalInput")
    bv_d = nc.dram_tensor("bv", [128, 8], F32, kind="ExternalInput")
    bq_d = nc.dram_tensor("bq", [128, 8], F32, kind="ExternalInput")
    ftk_d = nc.dram_tensor("ftk", [128, 8 * D], F32, kind="ExternalInput")
    ftvq_d = nc.dram_tensor("ftvq", [128, 8 * D], F32, kind="ExternalInput")
    ftq_d = nc.dram_tensor("ftq", [128, 8 * D], F32, kind="ExternalInput")
    gm_d = nc.dram_tensor("gmat", [128, 8 * D], F32, kind="ExternalInput")
    id_d = nc.dram_tensor("ident", [128, 128], F32, kind="ExternalInput")
    lt_d = nc.dram_tensor("ltri", [128, 128], F32, kind="ExternalInput")
    or_d = nc.dram_tensor("onesrow", [1, 128], F32, kind="ExternalInput")
    oc_d = nc.dram_tensor("onescol", [128, 1], F32, kind="ExternalInput")
    wia_d = nc.dram_tensor("w_inT_a", [16, 128, D], F32, kind="ExternalInput")
    wib_d = nc.dram_tensor("w_inT_b", [16, 128, D], F32, kind="ExternalInput")
    w1a_d = nc.dram_tensor("w1T_a", [16, 128, D], F32, kind="ExternalInput")
    w1b_d = nc.dram_tensor("w1T_b", [16, 128, D], F32, kind="ExternalInput")
    wo_d = nc.dram_tensor("w_outT", [128, 16 * D], F32, kind="ExternalInput")
    w2_d = nc.dram_tensor("w2T", [NE, 128, 16 * D], F32, kind="ExternalInput")
    wg_d = nc.dram_tensor("wgT", [128, 80], F32, kind="ExternalInput")
    gb_d = nc.dram_tensor("gbeb", [1, 10], F32, kind="ExternalInput")
    out_d = nc.dram_tensor("out", [T, D], F32, kind="ExternalOutput")
    x1s_d = nc.dram_tensor("x1s", [T, D], F32)
    wabs_d = nc.dram_tensor("wabs", [4, 16, 128, D], BF16)
    w2s_d = nc.dram_tensor("w2s", [NE + 1, 128, 16 * D], BF16)

    PE, DVE, ACT, SP, PL = nc.tensor, nc.vector, nc.scalar, nc.sync, nc.gpsimd

    with ExitStack() as glob:
        sem = lambda name: glob.enter_context(nc.semaphore(name))
        sb = lambda name, shape, dt: glob.enter_context(nc.sbuf_tensor(name, shape, dt))

        id_b = sb("id_b", [128, 128], BF16)
        lt_b = sb("lt_b", [128, 128], BF16)
        orow_b = sb("orow_b", [1, 128], BF16)
        ocol_b = sb("ocol_b", [128, 1], BF16)
        mask_f = sb("mask_f", [1, 1], F32)
        eps_c = sb("eps_c", [128, 1], F32)
        cry_b = sb("cry_b", [1, 2 * D], BF16)

        s_cst = sem("s_cst")
        s_cvt = sem("s_cvt")
        PL.dma_start(id_b[:, :], id_d[:, :]).then_inc(s_cst, 16)
        PL.dma_start(lt_b[:, :], lt_d[:, :]).then_inc(s_cst, 16)
        PL.dma_start(orow_b[:, :], or_d[:, :]).then_inc(s_cst, 16)
        PL.dma_start(ocol_b[:, :], oc_d[:, :]).then_inc(s_cst, 16)
        PL.dma_start(mask_f[:, :], mask_d[:, :]).then_inc(s_cst, 16)
        CST = 80
        s_ini = sem("s_ini")
        DVE.memset(eps_c[:, :], 1e-5)
        DVE.drain()
        DVE.engine_nop().then_inc(s_ini, 1)
        ACT.wait_ge(s_ini, 1)

        s_ft = sem("s_ft")
        s_wr = [sem(f"s_wr{i}") for i in range(2)]
        s_bc = sem("s_bc")
        sx = [sem(f"sx{i}") for i in range(6)]
        ss = [sem(f"ss{i}") for i in range(2)]
        st_f = sem("st_f")
        sc_f = sem("sc_f")
        sv_td = sem("sv_td")
        sv_st = sem("sv_st")
        sc_ln = sem("sc_ln")
        sv_ln = sem("sv_ln")
        st_xT = sem("st_xT")
        sc_xT = sem("sc_xT")
        st_k = sem("st_k")
        st_v = sem("st_v")
        st_q = sem("st_q")
        sc_k = sem("sc_k")
        sc_v = sem("sc_v")
        sc_cq = sem("sc_cq")
        st_cy = sem("st_cy")
        s_cy2 = [sem(f"s_cy2{i}") for i in range(2)]
        sv_kv = sem("sv_kv")
        st_cs = sem("st_cs")
        sv_cr = sem("sv_cr")
        sv_cy = sem("sv_cy")
        st_m = sem("st_m")
        sc_m = sem("sc_m")
        sv_m = sem("sv_m")
        st_cm = sem("st_cm")
        sv_q1 = sem("sv_q1")
        sc_nm = sem("sc_nm")
        sv_qn = sem("sv_qn")
        st_qT = sem("st_qT")
        sc_qT = sem("sc_qT")
        st_id = sem("st_id")
        sv_x1 = sem("sv_x1")

        with ExitStack() as stA:
            sbA = lambda name, shape, dt: stA.enter_context(nc.sbuf_tensor(name, shape, dt))
            wtk_b = sbA("wtk_b", [128, 8 * D], BF16)
            wtv_b = sbA("wtv_b", [128, 8 * D], BF16)
            wtq_b = sbA("wtq_b", [128, 8 * D], BF16)
            biasP = sbA("biasP", [1, 3 * D], BF16)

            ps_tr = stA.enter_context(nc.psum_tensor("ps_tr", [128, D], F32))
            ps_kv = stA.enter_context(nc.psum_tensor("ps_kv", [128, D], F32))
            ps_q = stA.enter_context(nc.psum_tensor("ps_q", [128, D], F32))

            xf = sbA("xf", [128, 4 * D], F32)
            xln_b = sbA("xln_b", [128, 2 * D], BF16)
            xT_b = sbA("xT_b", [128, 2 * D], BF16)
            stats = sbA("stats", [128, 12], F32)
            aggr = sbA("aggr", [128, 4], F32)
            rcol = sbA("rcol", [128, 4], F32)

            with ExitStack() as stF:
                sbF = lambda name, shape, dt: stF.enter_context(nc.sbuf_tensor(name, shape, dt))
                ps_cy = stF.enter_context(nc.psum_tensor("ps_cy", [1, D], F32))
                ftb = sbF("ftb", [128, 8 * D], BF16)
                wraw = sbF("wraw", [128, 8 * D], BF16)
                bcol = sbF("bcol", [128, 24], BF16)
                xln_all = sbF("xln_all", [128, NPT * D], BF16)
                m_sb = sbF("m_sb", [128, 8 * D], BF16)
                hh = xT_b
                xbr_row = sbF("xbr_row", [1, D], BF16)
                xbr_col = sbF("xbr_col", [128, 8], BF16)
                axcx = sbF("axcx", [1, 2 * D], BF16)
                s_sb = sbF("s_sb", [1, 4 * 512], BF16)
                tr1 = sbF("tr1", [1, 2 * 512], BF16)
                ps_cs = ps_cy

                PL.dma_start(bcol[:, 0:8], bk_d[:, :]).then_inc(s_bc, 16)
                PL.dma_start(bcol[:, 8:16], bv_d[:, :]).then_inc(s_bc, 16)
                PL.dma_start(bcol[:, 16:24], bq_d[:, :]).then_inc(s_bc, 16)

                nfold = 0
                for pi, (w_d, ft_d, wt_b) in enumerate(
                        [(wk_d, ftk_d, wtk_b), (wv_d, ftvq_d, wtv_b),
                         (wq_d, ftq_d, wtq_b)]):
                    if pi >= 1:
                        PL.wait_ge(st_f, 9 * pi)
                    PL.dma_start(ftb[:, :], ft_d[:, :]).then_inc(s_ft, 16)
                    PL.dma_start(wraw[:, :], w_d[:, :]).then_inc(s_ft, 16)
                    PE.wait_ge(s_ft, 32 * (pi + 1))
                    if pi == 0:
                        PE.wait_ge(s_cst, CST)
                        PE.wait_ge(s_bc, 48)
                    wbuf = wraw
                    for dc in range(8):
                        if nfold >= 2:
                            PE.wait_ge(sc_f, nfold - 1)
                        ps = ps_tr if (nfold % 2 == 0) else ps_kv
                        for jc in range(8):
                            for h in range(2):
                                mi = PE.matmul(
                                    ps[:, 512 * h:512 * (h + 1)],
                                    wbuf[:, jc * D + dc * 128: jc * D + (dc + 1) * 128],
                                    ftb[:, jc * D + 512 * h: jc * D + 512 * (h + 1)],
                                    start=(jc == 0), stop=(jc == 7))
                        mi.then_inc(st_f, 1)
                        nfold += 1
                        ACT.wait_ge(st_f, nfold)
                        ACT.copy(wt_b[:, dc * D:(dc + 1) * D], ps[:, :]).then_inc(sc_f, 1)
                    if pi > 0:
                        PE.wait_ge(sc_f, nfold)
                    for jc in range(8):
                        for h in range(2):
                            mi = PE.matmul(
                                ps_cs[:, 512 * h:512 * (h + 1)],
                                bcol[:, 8 * pi + jc: 8 * pi + jc + 1],
                                ftb[:, jc * D + 512 * h: jc * D + 512 * (h + 1)],
                                start=(jc == 0), stop=(jc == 7))
                    mi.then_inc(st_f, 1)
                    nfold += 1
                    ACT.wait_ge(st_f, nfold)
                    ACT.copy(biasP[0:1, pi * D:(pi + 1) * D], ps_cs[0:1, :]).then_inc(sc_f, 1)
                FOLD = nfold
                CVT = 16 * 13

                x_t = x_d.ap().rearrange("(n p) d -> n p d", p=128)
                xp_t = xp_d.ap().rearrange("(n p) d -> n p d", p=128)
                x1s_t = x1s_d.ap().rearrange("(n p) d -> n p d", p=128)
                GTOT = NPT + NT

                def xsrc(g):
                    return xf[:, (g % 4) * D:(g % 4 + 1) * D]

                def sp_load(g):
                    if g >= 4:
                        SP.wait_ge(sv_td, g - 3)
                    ap = xp_t[g] if g < NPT else x_t[g - NPT]
                    SP.dma_start(xsrc(g), ap).then_inc(sx[g % 4], 16)

                def d_stats(g):
                    DVE.wait_ge(sx[g % 4], 16 * (g // 4 + 1))
                    DVE.bn_stats(stats[:, 0:6], xsrc(g)[:, 0:512])
                    DVE.bn_stats(stats[:, 6:12], xsrc(g)[:, 512:1024])
                    DVE.drain()
                    DVE.bn_aggr(aggr[:, 2 * (g % 2):2 * (g % 2) + 2], stats[:, :])
                    DVE.drain()
                    DVE.engine_nop().then_inc(sv_st, 1)

                def a_lnexp(g):
                    r = g % 2
                    ACT.wait_ge(sv_st, g + 1)
                    if g >= 2:
                        ACT.wait_ge(sv_ln, g - 1)
                    ACT.activation(rcol[:, 2 * r + 1:2 * r + 2], aggr[:, 2 * r + 1:2 * r + 2],
                                   ACTF.Ln, bias=eps_c[:, 0:1])
                    ACT.drain()
                    ACT.activation(rcol[:, 2 * r:2 * r + 1], rcol[:, 2 * r + 1:2 * r + 2],
                                   ACTF.Exp, scale=-0.5)
                    ACT.drain()
                    ACT.nop().then_inc(sc_ln, 1)

                def d_apply(g):
                    r = g % 2
                    DVE.wait_ge(sc_ln, g + 1)
                    if g < NPT:
                        dst = xln_all[:, g * D:(g + 1) * D]
                    else:
                        t = g - NPT
                        if t >= 2:
                            DVE.wait_ge(st_xT, t - 1)
                        dst = xln_b[:, r * D:(r + 1) * D]
                    DVE.tensor_scalar(dst, xsrc(g),
                                      aggr[:, 2 * r:2 * r + 1], rcol[:, 2 * r:2 * r + 1],
                                      ALU.subtract, ALU.mult)
                    DVE.drain()
                    DVE.engine_nop().then_inc(sv_ln, 1)
                    if g < NPT:
                        DVE.engine_nop().then_inc(sv_td, 1)

                def ln_chain(g):
                    d_stats(g)
                    a_lnexp(g)
                    d_apply(g)

                def p_xbar(pt):
                    PE.wait_ge(sv_ln, pt + 1)
                    for h in range(2):
                        mi = PE.matmul(ps_cy[0:1, 512 * h:512 * (h + 1)], ocol_b[:, :],
                                       xln_all[:, pt * D + 512 * h: pt * D + 512 * (h + 1)],
                                       start=(pt == 0), stop=(pt == NPT - 1))
                    mi.then_inc(st_cy, 1)

                def p_trX(t):
                    PE.wait_ge(sv_ln, NPT + t + 1)
                    if t == 0:
                        PE.wait_ge(sv_m, MV_TOT[0])
                    else:
                        PE.wait_ge(sc_k, t)
                    xl = xln_b[:, (t % 2) * D:(t % 2 + 1) * D]
                    for cc in range(8):
                        mi = PE.matmul(ps_kv[:, cc * 128:(cc + 1) * 128],
                                       xl[:, cc * 128:(cc + 1) * 128], id_b[:, :],
                                       start=True, stop=True)
                    mi.then_inc(st_xT, 1)

                def a_evacX(t):
                    ACT.wait_ge(st_xT, t + 1)
                    if t >= 2:
                        ACT.wait_ge(st_v, t - 1)
                    ACT.copy(xT_b[:, (t % 2) * D:(t % 2 + 1) * D], ps_kv[:, :]).then_inc(sc_xT, 1)

                def _pj(ps, xt, wt, dc, h):
                    return PE.matmul(ps[:, 512 * h:512 * (h + 1)],
                                     xt[:, dc * 128:(dc + 1) * 128],
                                     wt[:, dc * D + 512 * h: dc * D + 512 * (h + 1)],
                                     start=(dc == 0), stop=False)

                def _pjbias(ps, bi, stop=True):
                    for h in range(2):
                        mi = PE.matmul(ps[:, 512 * h:512 * (h + 1)], orow_b[0:1, :],
                                       biasP[0:1, bi * D + 512 * h: bi * D + 512 * (h + 1)],
                                       start=False, stop=stop)
                    return mi

                def p_projK(t):
                    PE.wait_ge(sc_xT, t + 1)
                    xt = xT_b[:, (t % 2) * D:(t % 2 + 1) * D]
                    for dc in range(8):
                        for h in range(2):
                            _pj(ps_kv, xt, wtk_b, dc, h)
                    _pjbias(ps_kv, 0).then_inc(st_k, 1)

                def p_projQ(t):
                    if t == 0:
                        PE.wait_ge(sc_m, SCM_TOT[0])
                    else:
                        PE.wait_ge(sc_cq, t)
                    xt = xT_b[:, (t % 2) * D:(t % 2 + 1) * D]
                    for dc in range(8):
                        for h in range(2):
                            _pj(ps_q, xt, wtq_b, dc, h)
                    _pjbias(ps_q, 2).then_inc(st_q, 1)

                def a_evacK(t):
                    ACT.wait_ge(st_k, t + 1)
                    if t >= 1:
                        ACT.wait_ge(sv_kv, t)
                    ACT.copy(kvs_b[:, 0:D], ps_kv[:, :]).then_inc(sc_k, 1)

                def p_projV(t):
                    if t == 1:
                        PE.wait_ge(sc_v, 1)
                    elif t >= 2:
                        PE.wait_ge(sv_x1, t - 1)
                    xt = xT_b[:, (t % 2) * D:(t % 2 + 1) * D]
                    for dc in range(8):
                        for h in range(2):
                            _pj(ps_id, xt, wtv_b, dc, h)
                    _pjbias(ps_id, 1).then_inc(st_v, 1)

                def a_evacV(t):
                    ACT.wait_ge(st_v, t + 1)
                    ACT.copy(kvs_b[:, D:2 * D], ps_id[:, :]).then_inc(sc_v, 1)

                def cmul(dst_ap, pa, pb):
                    t1, t2 = prod[:, 0:512], prod[:, 512:1024]
                    t3, t4 = prod[:, 1024:1536], prod[:, 1536:2048]
                    DVE.tensor_mul(t1, pa[:, 0:512], pb[:, 0:512])
                    DVE.tensor_mul(t2, pa[:, 512:1024], pb[:, 512:1024])
                    DVE.tensor_mul(t3, pa[:, 0:512], pb[:, 512:1024])
                    DVE.tensor_mul(t4, pa[:, 512:1024], pb[:, 0:512])
                    DVE.drain()
                    DVE.tensor_sub(dst_ap[:, 1:512], t1[:, 1:512], t2[:, 1:512])
                    DVE.tensor_copy(dst_ap[:, 0:1], t1[:, 0:1])
                    DVE.tensor_add(dst_ap[:, 513:1024], t3[:, 1:512], t4[:, 1:512])
                    DVE.tensor_copy(dst_ap[:, 512:513], t2[:, 0:1])
                    DVE.drain()

                def d_products(t):
                    DVE.wait_ge(sc_v, t + 1)
                    if t >= 2:
                        DVE.wait_ge(st_cm, t - 1)
                    cmul(kv_b[:, (t % 2) * D:(t % 2 + 1) * D],
                         kvs_b[:, 0:D], kvs_b[:, D:2 * D])
                    DVE.engine_nop().then_inc(sv_kv, 1)

                def p_cum(t):
                    PE.wait_ge(sv_kv, t + 1)
                    if t == 0:
                        PE.wait_ge(sv_cr, 1)
                    else:
                        PE.wait_ge(s_cy2[t % 2], 16 * ((t + 1) // 2))
                        PE.wait_ge(sc_qT, t)
                    cry = cry_b[0:1, (t % 2) * D:(t % 2 + 1) * D]
                    kv = kv_b[:, (t % 2) * D:(t % 2 + 1) * D]
                    for h in range(2):
                        PE.matmul(ps_tr[:, 512 * h:512 * (h + 1)], orow_b[0:1, :],
                                  cry[0:1, 512 * h:512 * (h + 1)], start=True, stop=False)
                        mi = PE.matmul(ps_tr[:, 512 * h:512 * (h + 1)], lt_b[:, :],
                                       kv[:, 512 * h:512 * (h + 1)], start=False, stop=True)
                    mi.then_inc(st_cm, 1)

                def a_evacCQ(t):
                    ACT.wait_ge(st_cm, t + 1)
                    ACT.wait_ge(st_q, t + 1)
                    if t >= 1:
                        ACT.wait_ge(sv_q1, t)
                    ACT.copy(cqs_b[:, 0:D], ps_tr[:, :])
                    ACT.drain()
                    if t + 1 < NT:
                        ACT.dma_start(cry_b[0:1, ((t + 1) % 2) * D:((t + 1) % 2 + 1) * D],
                                      cqs_b[127:128, 0:D]).then_inc(s_cy2[(t + 1) % 2], 16)
                    ACT.copy(cqs_b[:, D:2 * D], ps_q[:, :]).then_inc(sc_cq, 1)

                def d_queried(t):
                    DVE.wait_ge(sc_cq, t + 1)
                    if t >= 1:
                        DVE.wait_ge(sc_nm, t)
                    cmul(qd_b, cqs_b[:, 0:D], cqs_b[:, D:2 * D])
                    DVE.engine_nop().then_inc(sv_q1, 1)

                def a_norm(t):
                    ACT.wait_ge(sv_q1, t + 1)
                    ACT.activation(sq_f[:, :], qd_b[:, :], ACTF.Square, accum_out=nrm[:, 0:1])
                    ACT.drain()
                    ACT.activation(nrm[:, 1:2], nrm[:, 0:1], ACTF.Ln)
                    ACT.drain()
                    ACT.activation(nrm[:, 2:3], nrm[:, 1:2], ACTF.Exp, scale=0.5)
                    ACT.drain()
                    ACT.nop().then_inc(sc_nm, 1)

                def d_qn(t):
                    DVE.wait_ge(sc_nm, t + 1)
                    DVE.tensor_scalar(nrm[:, 3:4], nrm[:, 2:3], 1e-9, None, ALU.add)
                    DVE.drain()
                    DVE.reciprocal(nrm[:, 3:4], nrm[:, 3:4])
                    DVE.drain()
                    if t >= 1:
                        DVE.wait_ge(st_qT, t)
                    DVE.tensor_scalar(qn_b[:, :], qd_b[:, :], nrm[:, 3:4], None, ALU.mult)
                    DVE.drain()
                    DVE.engine_nop().then_inc(sv_qn, 1)

                def p_trQ(t):
                    PE.wait_ge(sv_qn, t + 1)
                    PE.wait_ge(sc_cq, t + 1)
                    for cc in range(8):
                        mi = PE.matmul(ps_tr[:, cc * 128:(cc + 1) * 128],
                                       qn_b[:, cc * 128:(cc + 1) * 128], id_b[:, :],
                                       start=True, stop=True)
                    mi.then_inc(st_qT, 1)

                def a_evacQn(t):
                    ACT.wait_ge(st_qT, t + 1)
                    if t >= 1:
                        ACT.wait_ge(st_id, t)
                    ACT.copy(qnT_b[:, :], ps_tr[:, :]).then_inc(sc_qT, 1)

                def p_idft(t):
                    PE.wait_ge(sc_qT, t + 1)
                    if t == 0:
                        PE.wait_ge(s_ft, 112)
                    if t + 1 < NT:
                        PE.wait_ge(sc_v, t + 2)
                    else:
                        PE.wait_ge(sv_x1, t)
                    for fc in range(8):
                        for h in range(2):
                            mi = PE.matmul(ps_id[:, 512 * h:512 * (h + 1)],
                                           qnT_b[:, fc * 128:(fc + 1) * 128],
                                           gm_b[:, fc * D + 512 * h: fc * D + 512 * (h + 1)],
                                           start=(fc == 0), stop=(fc == 7))
                    mi.then_inc(st_id, 1)

                def d_x1(t):
                    g = NPT + t
                    DVE.wait_ge(st_id, t + 1)
                    if t >= 2:
                        DVE.wait_ge(ss[t % 2], 16 * (t // 2))
                    x1 = x1_sb[:, (t % 2) * D:(t % 2 + 1) * D]
                    DVE.tensor_add(x1, xsrc(g), ps_id[:, :])
                    DVE.drain()
                    DVE.engine_nop().then_inc(sv_x1, 1)
                    DVE.engine_nop().then_inc(sv_td, 1)

                SP.wait_ge(s_cst, CST)
                for g0 in range(min(3, GTOT)):
                    sp_load(g0)
                d_stats(0)
                a_lnexp(0)
                if GTOT > 1:
                    d_stats(1)
                    a_lnexp(1)
                for pt in range(NPT):
                    if pt + 3 < GTOT:
                        sp_load(pt + 3)
                    d_apply(pt)
                    if pt + 2 < GTOT:
                        d_stats(pt + 2)
                        a_lnexp(pt + 2)
                    p_xbar(pt)

                MV_TOT = [0]
                SCM_TOT = [0]
                nm_t, nm_a, nm_v = 0, 0, 0
                PE.wait_ge(sv_ln, NPT)
                for dc in range(8):
                    ps = ps_tr if (dc % 2 == 0) else ps_kv
                    if dc >= 2:
                        PE.wait_ge(sc_m, nm_a - 1)
                    for pt in range(NPT):
                        for h in range(2):
                            mi = PE.matmul(ps[:, 512 * h:512 * (h + 1)],
                                           xln_all[:, pt * D + dc * 128: pt * D + (dc + 1) * 128],
                                           xln_all[:, pt * D + 512 * h: pt * D + 512 * (h + 1)],
                                           start=(pt == 0), stop=(pt == NPT - 1))
                    mi.then_inc(st_m, 1)
                    nm_t += 1
                    ACT.wait_ge(st_m, nm_t)
                    ACT.copy(m_sb[:, dc * D:(dc + 1) * D], ps[:, :]).then_inc(sc_m, 1)
                    nm_a += 1
                ACT.wait_ge(st_cy, NPT)
                ACT.copy(xbr_row[0:1, :], ps_cy[0:1, :]).then_inc(sc_m, 1)
                nm_a += 1
                PE.wait_ge(sc_m, nm_a)
                for cc in range(8):
                    mi = PE.matmul(ps_q[:, cc:cc + 1], xbr_row[0:1, cc * 128:(cc + 1) * 128],
                                   orow_b[0:1, 0:1], start=True, stop=True)
                mi.then_inc(st_m, 1)
                nm_t += 1
                ACT.wait_ge(st_m, nm_t)
                ACT.copy(xbr_col[:, 0:8], ps_q[:, 0:8]).then_inc(sc_m, 1)
                nm_a += 1
                for wi, wt_b in enumerate((wtk_b, wtv_b)):
                    PE.wait_ge(sc_m, nm_a)
                    for cc in range(8):
                        for h in range(2):
                            mi = PE.matmul(ps_cy[0:1, 512 * h:512 * (h + 1)],
                                           xbr_col[:, cc:cc + 1],
                                           wt_b[:, cc * D + 512 * h: cc * D + 512 * (h + 1)],
                                           start=(cc == 0), stop=(cc == 7))
                    mi.then_inc(st_m, 1)
                    nm_t += 1
                    ACT.wait_ge(st_m, nm_t)
                    ACT.copy(axcx[0:1, wi * D:(wi + 1) * D], ps_cy[0:1, :]).then_inc(sc_m, 1)
                    nm_a += 1
                for dc in range(8):
                    psG = ps_tr if (dc % 2 == 0) else ps_kv
                    if dc >= 2:
                        PE.wait_ge(sv_m, nm_v - 1)
                    for c8 in range(8):
                        for h in range(2):
                            mi = PE.matmul(psG[:, 512 * h:512 * (h + 1)],
                                           m_sb[:, c8 * D + dc * 128: c8 * D + (dc + 1) * 128],
                                           wtv_b[:, c8 * D + 512 * h: c8 * D + 512 * (h + 1)],
                                           start=(c8 == 0), stop=(c8 == 7))
                    mi.then_inc(st_m, 1)
                    nm_t += 1
                    DVE.wait_ge(st_m, nm_t)
                    DVE.tensor_mul(hh[:, 0:512], wtk_b[:, dc * D:dc * D + 512], psG[:, 0:512])
                    DVE.tensor_mul(hh[:, 512:1024], wtk_b[:, dc * D + 512:dc * D + 1024],
                                   psG[:, 512:1024])
                    DVE.tensor_mul(hh[:, 1024:1536], wtk_b[:, dc * D:dc * D + 512],
                                   psG[:, 512:1024])
                    DVE.tensor_mul(hh[:, 1536:2048], wtk_b[:, dc * D + 512:dc * D + 1024],
                                   psG[:, 0:512])
                    DVE.drain()
                    DVE.engine_nop().then_inc(sv_m, 1)
                    nm_v += 1
                    PE.wait_ge(sv_m, nm_v)
                    if dc == 0:
                        PE.wait_ge(sc_m, nm_a)
                    PE.matmul(ps_cy[0:1, 0:512], ocol_b[:, :], hh[:, 0:512],
                              start=(dc == 0), stop=(dc == 7))
                    PE.matmul(ps_cy[0:1, 512:1024], ocol_b[:, :], hh[:, 512:1024],
                              start=(dc == 0), stop=(dc == 7))
                    PE.matmul(ps_q[0:1, 0:512], ocol_b[:, :], hh[:, 1024:1536],
                              start=(dc == 0), stop=(dc == 7))
                    mi = PE.matmul(ps_q[0:1, 512:1024], ocol_b[:, :], hh[:, 1536:2048],
                                   start=(dc == 0), stop=(dc == 7))
                    if dc == 7:
                        mi.then_inc(st_m, 1)
                        nm_t += 1
                ACT.wait_ge(st_m, nm_t)
                ACT.copy(s_sb[0:1, 0:1024], ps_cy[0:1, :])
                ACT.drain()
                ACT.copy(s_sb[0:1, 1024:2048], ps_q[0:1, 0:1024]).then_inc(sc_m, 1)
                nm_a += 1
                DVE.wait_ge(sc_m, nm_a)
                for j, (A, B) in enumerate(((0, 0), (1, 1), (0, 1), (1, 0))):
                    bk = biasP[0:1, A * 512:(A + 1) * 512]
                    bv = biasP[0:1, D + B * 512:D + (B + 1) * 512]
                    cx = axcx[0:1, D + B * 512:D + (B + 1) * 512]
                    ax = axcx[0:1, A * 512:(A + 1) * 512]
                    t0 = tr1[0:1, 0:512]
                    DVE.tensor_mul(t0, bk, cx)
                    DVE.drain()
                    DVE.tensor_add(s_sb[0:1, j * 512:(j + 1) * 512],
                                   s_sb[0:1, j * 512:(j + 1) * 512], t0)
                    DVE.drain()
                    DVE.tensor_mul(t0, bv, ax)
                    DVE.drain()
                    DVE.tensor_add(s_sb[0:1, j * 512:(j + 1) * 512],
                                   s_sb[0:1, j * 512:(j + 1) * 512], t0)
                    DVE.drain()
                    DVE.tensor_mul(t0, bk, bv)
                    DVE.drain()
                    DVE.scalar_tensor_tensor(s_sb[0:1, j * 512:(j + 1) * 512], t0,
                                             float(NPT * 128),
                                             s_sb[0:1, j * 512:(j + 1) * 512],
                                             ALU.mult, ALU.add)
                    DVE.drain()
                DVE.tensor_sub(tr1[0:1, 1:512], s_sb[0:1, 1:512], s_sb[0:1, 513:1024])
                DVE.tensor_copy(tr1[0:1, 0:1], s_sb[0:1, 0:1])
                DVE.tensor_copy(tr1[0:1, 512:513], s_sb[0:1, 512:513])
                DVE.tensor_add(tr1[0:1, 513:1024], s_sb[0:1, 1025:1536], s_sb[0:1, 1537:2048])
                DVE.drain()
                DVE.tensor_scalar(cry_b[0:1, 0:D], tr1[0:1, 0:1024],
                                  mask_f[0:1, 0:1], None, ALU.mult)
                DVE.drain()
                DVE.engine_nop().then_inc(sv_cr, 1)
                MV_TOT[0] = nm_v
                SCM_TOT[0] = nm_a
            ps_id = stA.enter_context(nc.psum_tensor("ps_id", [128, D], F32))
            gm_b = sbA("gm_b", [128, 8 * D], BF16)
            PL.wait_ge(sc_f, FOLD)
            PL.dma_start(gm_b[:, :], gm_d[:, :]).then_inc(s_ft, 16)
            PL.wait_ge(sv_cr, 1)
            for wi, wsrc in enumerate((wia_d, wib_d, w1a_d, w1b_d)):
                PL.dma_start(wabs_d.ap()[wi], wsrc.ap()).then_inc(s_cvt, 16)
            PL.dma_start(w2s_d.ap()[0], wo_d.ap()).then_inc(s_cvt, 16)
            for e in range(NE):
                PL.dma_start(w2s_d.ap()[1 + e], w2_d.ap()[e]).then_inc(s_cvt, 16)
            qd_b = sbA("qd_b", [128, D], BF16)
            qn_b = sbA("qn_b", [128, D], BF16)
            qnT_b = sbA("qnT_b", [128, D], BF16)
            x1_sb = sbA("x1_sb", [128, 2 * D], F32)
            nrm = sbA("nrm", [128, 4], F32)
            prod = sbA("prod", [128, 4 * 512], BF16)
            kv_b = sbA("kv_b", [128, 2 * D], BF16)
            kvs_b = sbA("kvs_b", [128, 2 * D], BF16)
            cqs_b = sbA("cqs_b", [128, 2 * D], BF16)
            sq_f = sbA("sq_f", [128, D], F32)

            d_apply(NPT)
            if NT > 1:
                d_apply(NPT + 1)
            p_trX(0)
            a_evacX(0)
            p_projK(0)
            a_evacK(0)
            p_projQ(0)
            p_projV(0)
            a_evacV(0)
            d_products(0)
            for t in range(NT):
                g = NPT + t
                if g + 3 < GTOT:
                    sp_load(g + 3)
                p_cum(t)
                if t + 1 < NT:
                    p_trX(t + 1)
                if t >= 1:
                    p_idft(t - 1)
                if g + 2 < GTOT:
                    ln_chain(g + 2)
                if t >= 1:
                    d_x1(t - 1)
                d_queried(t)
                if t + 1 < NT:
                    a_evacX(t + 1)
                a_evacCQ(t)
                if t + 1 < NT:
                    p_projK(t + 1)
                    a_evacK(t + 1)
                a_norm(t)
                d_qn(t)
                if t + 1 < NT:
                    p_projV(t + 1)
                    a_evacV(t + 1)
                p_trQ(t)
                if t + 1 < NT:
                    p_projQ(t + 1)
                a_evacQn(t)
                if t + 1 < NT:
                    d_products(t + 1)
                if t >= 1:
                    SP.wait_ge(sv_x1, t)
                    SP.dma_start(x1s_t[t - 1, :, :],
                                 x1_sb[:, ((t - 1) % 2) * D:((t - 1) % 2 + 1) * D]
                                 ).then_inc(ss[(t - 1) % 2], 16)
            p_idft(NT - 1)
            d_x1(NT - 1)
            SP.wait_ge(sv_x1, NT)
            SP.dma_start(x1s_t[NT - 1, :, :],
                         x1_sb[:, ((NT - 1) % 2) * D:((NT - 1) % 2 + 1) * D]
                         ).then_inc(ss[(NT - 1) % 2], 16)

        s_ac = sem("s_ac")
        s_out = sem("s_out")
        s_wg = sem("s_wg")
        s_wa = [sem(f"s_wa{i}") for i in range(2)]
        s_wb = [sem(f"s_wb{i}") for i in range(2)]
        s_x1t = [sem(f"s_x1t{i}") for i in range(2)]
        s_w2 = [sem(f"s_w2{i}") for i in range(2)]
        st_tb = sem("st_tb")
        sc_ev = sem("sc_ev")
        st_g = sem("st_g")
        sc_gx = sem("sc_gx")
        sv_g = sem("sv_g")
        sv_se = sem("sv_se")
        sc_gl = sem("sc_gl")
        st_ab = sem("st_ab")
        sc_sil = sem("sc_sil")
        sv_sT = sem("sv_sT")
        st_r = sem("st_r")
        sv_acc = sem("sv_acc")
        sv_fin = sem("sv_fin")

        SS0_TOT = 16 * ((NT + 1) // 2)
        SS1_TOT = 16 * (NT // 2)

        with ExitStack() as stB:
            sbB = lambda name, shape, dt: stB.enter_context(nc.sbuf_tensor(name, shape, dt))
            acc = sbB("acc", [128, TPC * D], F32)
            hiT = sbB("hiT", [128, 8 * CH], BF16)
            hsT = sbB("hsT", [128, 16 * CH], BF16)
            w2_b = [sbB(f"w2_{i}", [128, 16 * D], BF16) for i in range(2)]
            wab_b = [sbB(f"wab_{i}", [128, 2 * D], BF16) for i in range(2)]
            wg_hi = sbB("wg_hi", [128, 80], BF16)
            wg_f = sbB("wg_f", [128, 80], F32)
            gbeb_f = sbB("gbeb_f", [1, 10], F32)
            gbeb_b = sbB("gbeb_b", [1, 20], BF16)
            wse = sbB("wse", [128, NE * TPC], F32)
            hi_b = sbB("hi_b", [128, TPC * D], BF16)
            x1t = sbB("x1t", [128, 2 * D], F32)
            sil_b = sbB("sil_b", [128, 2 * D], BF16)
            st2 = sbB("st2", [128, 12], F32)
            ag2 = sbB("ag2", [128, 2 * TPC], F32)
            rc2 = sbB("rc2", [128, 2 * TPC], F32)
            gc = sbB("gc", [128, 32], F32)
            gb8 = sbB("gb8", [128, 96], F32)
            ge2 = sbB("ge2", [128, 4], F32)
            glog = sbB("glog", [128, 80], F32)

            ps_tb = stB.enter_context(nc.psum_tensor("ps_tb", [128, D], F32))
            ps_a = stB.enter_context(nc.psum_tensor("ps_a", [128, D], F32))
            ps_b = stB.enter_context(nc.psum_tensor("ps_b", [128, D], F32))
            ps_r = stB.enter_context(nc.psum_tensor("ps_r", [128, D], F32))

            PL.wait_ge(sv_x1, NT)
            PL.wait_ge(st_id, NT)
            PL.wait_ge(sc_qT, NT)
            PL.wait_ge(ss[0], SS0_TOT)
            PL.wait_ge(ss[1], SS1_TOT)
            PL.wait_ge(s_cy2[0], 16 * ((NT - 2) // 2))
            PL.wait_ge(s_cy2[1], 16 * (NT // 2))
            PL.dma_start(wg_f[:, :], wg_d[:, :]).then_inc(s_wg, 16)
            PL.dma_start(gbeb_f[:, :], gb_d[:, :]).then_inc(s_wg, 16)
            DVE.wait_ge(s_wg, 32)
            DVE.tensor_copy(wg_hi[:, :], wg_f[:, :])
            DVE.drain()
            DVE.tensor_copy(gbeb_b[0:1, 0:10], gbeb_f[0:1, :])
            DVE.drain()
            DVE.tensor_sub(gbeb_b[0:1, 10:20], gbeb_f[0:1, :], gbeb_b[0:1, 0:10])
            DVE.drain()
            DVE.engine_nop().then_inc(sv_g, 1)

            SP.wait_ge(sv_x1, NT)
            SP.wait_ge(st_id, NT)
            SP.wait_ge(sc_qT, NT)
            SP.wait_ge(ss[0], SS0_TOT)
            SP.wait_ge(ss[1], SS1_TOT)
            SP.wait_ge(s_cy2[0], 16 * ((NT - 2) // 2))
            SP.wait_ge(s_cy2[1], 16 * (NT // 2))

            out_r = out_d.ap().rearrange("(ch tl p) d -> ch p tl d", p=128, tl=TPC)
            out_t4 = out_d.ap().rearrange("(ch tl p) d -> ch tl p d", p=128, tl=TPC)
            x1r = x1s_d.ap().rearrange("(ch tl p) d -> ch p tl d", p=128, tl=TPC)
            acc3 = acc.ap().rearrange("p (tl d) -> p tl d", d=D)

            RPC = TPC + NE * TPC
            ctr = {"g": 1, "tb": 0, "gx": 0}
            gx_done = {}
            gx_ln = {}
            ev_tile = {}
            g_hl = {}
            n_ab = 0
            n_r = 0
            for c in range(NCH):
                def s1_chain(cc2, i):
                    G = cc2 * TPC + i
                    r = G % 2
                    src2 = x1t[:, r * D:(r + 1) * D]
                    if G >= 2:
                        SP.wait_ge(sv_g, g_hl[G - 2])
                    SP.dma_start(src2, x1s_d.ap().rearrange("(n p) d -> n p d", p=128)[G]
                                 ).then_inc(s_x1t[r], 16)
                    DVE.wait_ge(s_x1t[r], 16 * (G // 2 + 1))
                    if G >= TPC:
                        DVE.wait_ge(st_tb, G - TPC + 1)
                    DVE.bn_stats(st2[:, 0:6], src2[:, 0:512])
                    DVE.bn_stats(st2[:, 6:12], src2[:, 512:1024])
                    DVE.drain()
                    DVE.bn_aggr(ag2[:, 2 * i:2 * i + 2], st2[:, :])
                    DVE.drain()
                    DVE.engine_nop().then_inc(sv_g, 1)
                    ctr["g"] += 1
                    ACT.wait_ge(sv_g, ctr["g"])
                    ACT.activation(rc2[:, 2 * i + 1:2 * i + 2], ag2[:, 2 * i + 1:2 * i + 2],
                                   ACTF.Ln, bias=eps_c[:, 0:1])
                    ACT.drain()
                    ACT.activation(rc2[:, 2 * i:2 * i + 1], rc2[:, 2 * i + 1:2 * i + 2],
                                   ACTF.Exp, scale=-0.5)
                    ACT.drain()
                    ACT.nop().then_inc(sc_gx, 1)
                    ctr["gx"] += 1
                    gx_ln[G] = ctr["gx"]
                    DVE.wait_ge(sc_gx, ctr["gx"])
                    DVE.tensor_scalar(hi_b[:, i * D:(i + 1) * D], src2, ag2[:, 2 * i:2 * i + 1],
                                      rc2[:, 2 * i:2 * i + 1], ALU.subtract, ALU.mult)
                    DVE.drain()
                    DVE.engine_nop().then_inc(sv_g, 1)
                    ctr["g"] += 1
                    g_hl[G] = ctr["g"]

                def s1_pe(i):
                    G = c * TPC + i
                    q = i % 2
                    tr_ps = ps_tb if (i % 2 == 0) else ps_a
                    PE.wait_ge(sv_g, g_hl[G])
                    if c > 0 and i == 0:
                        PE.wait_ge(sv_acc, c * RPC)
                        PE.wait_ge(sv_sT, 32 * c)
                        PE.wait_ge(sc_sil, 32 * c)
                    if ctr["tb"] >= 2:
                        PE.wait_ge(sc_ev, ctr["tb"] - 1)
                    for dc in range(8):
                        mi = PE.matmul(tr_ps[:, dc * 128:(dc + 1) * 128],
                                       hi_b[:, i * D + dc * 128: i * D + (dc + 1) * 128],
                                       id_b[:, :], start=True, stop=True)
                    mi.then_inc(st_tb, 1)
                    ctr["tb"] += 1
                    DVE.wait_ge(st_tb, ctr["tb"])
                    if i == 0:
                        DVE.wait_ge(st_ab, 32 * c)
                        DVE.wait_ge(st_g, c * TPC)
                    DVE.tensor_copy(
                        hiT.ap().rearrange("p (dc n) -> p dc n", dc=8)[:, :, i * 128:(i + 1) * 128],
                        tr_ps.ap().rearrange("p (dc n) -> p dc n", dc=8),
                    )
                    DVE.drain()
                    DVE.engine_nop().then_inc(sc_ev, 1)
                    ev_tile[G] = ctr["tb"]
                def s1_gate_mm(i):
                    G = c * TPC + i
                    w = ps_b[:, 10 * i:10 * i + 10]
                    PE.wait_ge(sc_ev, ev_tile[G])
                    if G == 0:
                        PE.wait_ge(s_wg, 32)
                        PE.wait_ge(sv_g, 1)
                    for dc in range(8):
                        PE.matmul(w,
                                  hiT[:, dc * CH + i * 128: dc * CH + (i + 1) * 128],
                                  wg_hi[:, dc * 10:(dc + 1) * 10],
                                  start=(dc == 0), stop=False)
                    PE.matmul(w, orow_b[0:1, :], gbeb_b[0:1, 0:10],
                              start=False, stop=False)
                    PE.matmul(w, orow_b[0:1, :], gbeb_b[0:1, 10:20],
                              start=False, stop=True).then_inc(st_g, 1)

                def s1_gate(i):
                    G = c * TPC + i
                    q = i % 2
                    gl = glog[:, 10 * i:10 * i + 10]
                    DVE.wait_ge(sc_gl, c + 1)
                    el = gl[:, 2:10]
                    gcq = gc[:, 16 * q:16 * q + 16]
                    gbq = gb8[:, 48 * q:48 * q + 48]
                    msk, eex = gbq[:, 0:8], gbq[:, 8:16]
                    p8, t8 = gbq[:, 16:24], gbq[:, 24:32]
                    p2, k8 = gbq[:, 32:40], gbq[:, 40:48]
                    DVE.tensor_tensor(gcq[:, 0:1], gl[:, 1:2], gl[:, 0:1], ALU.is_gt)
                    DVE.reduce_max(gcq[:, 1:2], gl[:, 0:2], axis=AX.X)
                    DVE.drain()
                    DVE.tensor_scalar(gcq[:, 1:2], gcq[:, 1:2], -1.0, None, ALU.mult)
                    DVE.tensor_scalar(gcq[:, 4:5], gcq[:, 0:1], -BIG, None, ALU.mult)
                    DVE.tensor_scalar(gcq[:, 5:6], gcq[:, 0:1], 1.0, BIG, ALU.subtract, ALU.mult)
                    DVE.drain()
                    DVE.tensor_scalar(msk[:, 0:4], el[:, 0:4], gcq[:, 4:5], None, ALU.add)
                    DVE.tensor_scalar(msk[:, 4:8], el[:, 4:8], gcq[:, 5:6], None, ALU.add)
                    DVE.drain()
                    DVE.reduce_max(gcq[:, 6:7], msk[:, :], axis=AX.X)
                    DVE.drain()
                    DVE.tensor_scalar(gcq[:, 6:7], gcq[:, 6:7], -1.0, None, ALU.mult)
                    DVE.drain()
                    DVE.engine_nop().then_inc(sv_g, 1)
                    ctr["g"] += 1
                    ACT.wait_ge(sv_g, ctr["g"])
                    if G >= 1:
                        ACT.wait_ge(sv_se, G)
                    ACT.activation(ge2[:, 2 * q:2 * q + 2], gl[:, 0:2], ACTF.Exp,
                                   bias=gcq[:, 1:2])
                    ACT.drain()
                    ACT.activation(eex[:, :], msk[:, :], ACTF.Exp, bias=gcq[:, 6:7])
                    ACT.drain()
                    ACT.nop().then_inc(sc_gx, 2)
                    ctr["gx"] += 2
                    DVE.wait_ge(sc_gx, ctr["gx"])
                    DVE.reduce_sum(gcq[:, 2:3], ge2[:, 2 * q:2 * q + 2], axis=AX.X)
                    DVE.reduce_sum(gcq[:, 7:8], eex[:, :], axis=AX.X)
                    DVE.drain()
                    DVE.reciprocal(gcq[:, 3:4], gcq[:, 2:3])
                    DVE.reciprocal(gcq[:, 8:9], gcq[:, 7:8])
                    DVE.drain()
                    DVE.tensor_mul(gcq[:, 9:10], gcq[:, 8:9], gcq[:, 3:4])
                    DVE.drain()
                    DVE.tensor_scalar(p8[:, :], eex[:, :], gcq[:, 9:10], None, ALU.mult)
                    DVE.drain()
                    DVE.reduce_max(gcq[:, 10:11], p8[:, :], axis=AX.X)
                    DVE.drain()
                    DVE.tensor_scalar(t8[:, :], p8[:, :], gcq[:, 10:11], None, ALU.is_ge)
                    DVE.drain()
                    DVE.scalar_tensor_tensor(p2[:, :], t8[:, :], -BIG, p8[:, :],
                                             ALU.mult, ALU.add)
                    DVE.drain()
                    DVE.reduce_max(gcq[:, 11:12], p2[:, :], axis=AX.X)
                    DVE.drain()
                    DVE.tensor_scalar(k8[:, :], p8[:, :], gcq[:, 11:12], None, ALU.is_ge)
                    DVE.drain()
                    DVE.tensor_mul(wse[:, i * NE:(i + 1) * NE], k8[:, :], p8[:, :])
                    DVE.drain()
                    DVE.engine_nop().then_inc(sv_se, 1)

                def wout_load(cc):
                    kw2 = cc * 9
                    bw2 = kw2 % 2
                    if kw2 == 0:
                        PL.wait_ge(s_cvt, CVT)
                    if kw2 >= 2:
                        a2, j2 = divmod(kw2 - 2, 9)
                        PL.wait_ge(st_r, a2 * RPC + (j2 + 1) * TPC)
                    for h2 in range(2):
                        PL.dma_start(w2_b[bw2][:, h2 * 8 * D:(h2 + 1) * 8 * D],
                                     w2s_d.ap()[0, :, h2 * 8 * D:(h2 + 1) * 8 * D]
                                     ).then_inc(s_w2[bw2], 16)

                if c == 0:
                    for i in range(TPC):
                        s1_chain(0, i)
                for i in range(TPC):
                    s1_pe(i)
                for i in range(TPC):
                    s1_gate_mm(i)
                DVE.wait_ge(st_g, TPC * (c + 1))
                if c > 0:
                    DVE.wait_ge(sc_gx, gx_done[c - 1])
                DVE.tensor_copy(glog[:, :], ps_b[:, 0:10 * TPC])
                DVE.drain()
                DVE.engine_nop().then_inc(sc_gl, 1)

                if c > 0:
                    ACT.wait_ge(s_out, 128 * c)
                    ACT.wait_ge(sv_fin, c)
                ACT.dma_start(acc3, x1r[c], ).then_inc(s_ac, 16)

                for kind in range(2):
                    wa_d, wb_d2 = (wabs_d.ap()[2 * kind], wabs_d.ap()[2 * kind + 1])
                    for hc in range(16):
                        gidx = (c * 2 + kind) * 16 + hc
                        bi = gidx % 2
                        pa, pb = (ps_a, ps_b) if (gidx % 2 == 0) else (ps_tb, ps_r)
                        sil = sil_b[:, (gidx % 2) * D:(gidx % 2) * D + CH]
                        if gidx == 0:
                            SP.wait_ge(s_cvt, CVT)
                            PL.wait_ge(s_cvt, CVT)
                        if gidx >= 2:
                            SP.wait_ge(sv_sT, gidx - 1)
                            PL.wait_ge(sv_sT, gidx - 1)
                        SP.dma_start(wab_b[bi][:, 0:D], wa_d[hc]).then_inc(s_wa[bi], 16)
                        PL.dma_start(wab_b[bi][:, D:2 * D], wb_d2[hc]).then_inc(s_wb[bi], 16)
                        PE.wait_ge(s_wa[bi], 16 * (gidx // 2 + 1))
                        if gidx >= 2:
                            PE.wait_ge(sv_sT, gidx - 1)
                        if kind == 0 and hc == 0:
                            PE.wait_ge(sc_gl, c + 1)
                        if kind == 0 and hc <= 1:
                            PE.wait_ge(sc_ev, TPC * (c + 1))
                        if kind == 1 and hc <= 1:
                            PE.wait_ge(sv_acc, c * RPC + TPC)
                        WL = min(512, CH)
                        for dc in range(8):
                            for h in range(CH // WL):
                                PE.matmul(pa[:, WL * h:WL * (h + 1)],
                                          wab_b[bi][:, dc * 128:(dc + 1) * 128],
                                          hiT[:, dc * CH + WL * h: dc * CH + WL * (h + 1)],
                                          start=(dc == 0), stop=(dc == 7))
                        PE.wait_ge(s_wb[bi], 16 * (gidx // 2 + 1))
                        for dc in range(8):
                            for h in range(CH // WL):
                                mi = PE.matmul(pb[:, WL * h:WL * (h + 1)],
                                               wab_b[bi][:, D + dc * 128: D + (dc + 1) * 128],
                                               hiT[:, dc * CH + WL * h: dc * CH + WL * (h + 1)],
                                               start=(dc == 0), stop=(dc == 7))
                        mi.then_inc(st_ab, 1)
                        n_ab += 1
                        ACT.wait_ge(st_ab, n_ab)
                        ACT.activation(sil, pa[:, 0:CH], ACTF.Silu)
                        ACT.drain()
                        ACT.nop().then_inc(sc_sil, 1)
                        DVE.wait_ge(sc_sil, n_ab)
                        if hc == 0:
                            if kind == 0 and c > 0:
                                DVE.wait_ge(st_r, c * RPC)
                            if kind == 1:
                                DVE.wait_ge(st_r, c * RPC + TPC)
                        DVE.tensor_mul(hsT[:, hc * CH:(hc + 1) * CH], sil, pb[:, 0:CH])
                        DVE.drain()
                        DVE.engine_nop().then_inc(sv_sT, 1)
                    if kind == 0:
                        for i2 in range(TPC):
                            s1_gate(i2)
                        gx_done[c] = ctr["gx"]
                        kw = c * 9
                        bw = kw % 2
                        if c == 0:
                            wout_load(0)
                        for i in range(TPC):
                            pr = ps_r if (n_r % 2 == 0) else ps_tb
                            PE.wait_ge(sv_sT, c * 32 + 16)
                            PE.wait_ge(s_w2[bw], 32 * (kw // 2 + 1))
                            if n_r >= 2:
                                PE.wait_ge(sv_acc, n_r - 1)
                            for hc in range(16):
                                for h in range(2):
                                    mi = PE.matmul(pr[:, 512 * h:512 * (h + 1)],
                                                   hsT[:, hc * CH + i * 128: hc * CH + (i + 1) * 128],
                                                   w2_b[bw][:, hc * D + 512 * h: hc * D + 512 * (h + 1)],
                                                   start=(hc == 0), stop=(hc == 15))
                            mi.then_inc(st_r, 1)
                            n_r += 1
                            DVE.wait_ge(st_r, n_r)
                            DVE.wait_ge(s_ac, 16 * (c + 1))
                            DVE.tensor_add(acc[:, i * D:(i + 1) * D],
                                           acc[:, i * D:(i + 1) * D], pr[:, :])
                            DVE.drain()
                            DVE.engine_nop().then_inc(sv_acc, 1)
                for e in range(NE):
                    if c + 1 < NCH and e < TPC:
                        s1_chain(c + 1, e)
                    gw = c * 9 + 1 + e
                    bi = gw % 2
                    if gw >= 2:
                        cc, jj = divmod(gw - 2, 9)
                        PL.wait_ge(st_r, cc * RPC + (jj + 1) * TPC)
                    for h2 in range(2):
                        PL.dma_start(w2_b[bi][:, h2 * 8 * D:(h2 + 1) * 8 * D],
                                     w2s_d.ap()[1 + e, :, h2 * 8 * D:(h2 + 1) * 8 * D]
                                     ).then_inc(s_w2[bi], 16)
                    for i in range(TPC):
                        pr = ps_r if (n_r % 2 == 0) else ps_tb
                        PE.wait_ge(s_w2[bi], 32 * (gw // 2 + 1))
                        if n_r >= 2:
                            PE.wait_ge(sv_acc, n_r - 1)
                        if e == 0 and i == 0:
                            PE.wait_ge(sv_sT, c * 32 + 32)
                        for hc in range(16):
                            for h in range(2):
                                mi = PE.matmul(pr[:, 512 * h:512 * (h + 1)],
                                               hsT[:, hc * CH + i * 128: hc * CH + (i + 1) * 128],
                                               w2_b[bi][:, hc * D + 512 * h: hc * D + 512 * (h + 1)],
                                               start=(hc == 0), stop=(hc == 15))
                        mi.then_inc(st_r, 1)
                        n_r += 1
                        DVE.wait_ge(st_r, n_r)
                        DVE.wait_ge(sv_se, c * TPC + i + 1)
                        DVE.scalar_tensor_tensor(acc[:, i * D:(i + 1) * D], pr[:, :],
                                                 wse[:, i * NE + e: i * NE + e + 1],
                                                 acc[:, i * D:(i + 1) * D],
                                                 ALU.mult, ALU.add)
                        DVE.drain()
                        DVE.engine_nop().then_inc(sv_acc, 1)
                        if e == NE - 1:
                            ACT.wait_ge(sv_acc, n_r)
                            ACT.dma_start(out_t4[c, i], acc[:, i * D:(i + 1) * D]
                                          ).then_inc(s_out, 16)
                if c + 1 < NCH:
                    wout_load(c + 1)
                DVE.engine_nop().then_inc(sv_fin, 1)
            SP.wait_ge(s_out, 128 * NCH)
    return nc




def kernel(L0_attn_x, L0_attn_Wk, L0_attn_bk, L0_attn_Wv, L0_attn_bv,
           L0_attn_Wq, L0_attn_bq, L0_ffn_w_in, L0_ffn_w_out, L0_ffn_w1,
           L0_ffn_w2, L0_ffn_wg, L0_ffn_we, L0_ffn_gb, L0_ffn_eb):
    Bsz, S, _ = L0_attn_x.shape
    T = S // 2
    xf = np.ascontiguousarray(np.asarray(L0_attn_x, dtype=np.float32)).reshape(Bsz * S, D)
    consts = _make_consts()

    def colmaj(v):
        return np.ascontiguousarray(np.asarray(v, dtype=np.float32).reshape(8, 128).T)

    w_in = np.asarray(L0_ffn_w_in, dtype=np.float32)
    w1 = np.asarray(L0_ffn_w1, dtype=np.float32)
    w_out = np.asarray(L0_ffn_w_out, dtype=np.float32)
    w2 = np.asarray(L0_ffn_w2, dtype=np.float32)
    wg = np.asarray(L0_ffn_wg, dtype=np.float32)
    we = np.asarray(L0_ffn_we, dtype=np.float32)
    wgT = np.concatenate([wg, we], axis=0).T

    shared_w = {
        "Wk": _chunkmaj(np.asarray(L0_attn_Wk, dtype=np.float32)),
        "Wv": _chunkmaj(np.asarray(L0_attn_Wv, dtype=np.float32)),
        "Wq": _chunkmaj(np.asarray(L0_attn_Wq, dtype=np.float32)),
        "bk": colmaj(L0_attn_bk), "bv": colmaj(L0_attn_bv), "bq": colmaj(L0_attn_bq),
        "w_inT_a": _pieces_16(w_in[0:HID]),
        "w_inT_b": _pieces_16(w_in[HID:2 * HID]),
        "w1T_a": _pieces_16(w1[0:HID]),
        "w1T_b": _pieces_16(w1[HID:2 * HID]),
        "w_outT": _chunkmaj(np.ascontiguousarray(w_out.T)),
        "w2T": np.ascontiguousarray(np.stack(
            [_chunkmaj(np.ascontiguousarray(w2[e].T)) for e in range(NE)])),
        "wgT": _chunkmaj(np.ascontiguousarray(wgT)),
        "gbeb": np.ascontiguousarray(np.concatenate(
            [np.asarray(L0_ffn_gb, np.float32),
             np.asarray(L0_ffn_eb, np.float32)]).reshape(1, 10)),
        **consts,
    }

    nc = build_nc(T=T, TP=T, CH=1024)
    in_maps = []
    for c in range(NCORES):
        bidx, half = c // 2, c % 2
        base = bidx * S
        m = dict(shared_w)
        m["x"] = np.ascontiguousarray(xf[base + half * T: base + (half + 1) * T])
        m["xp"] = np.ascontiguousarray(xf[base: base + T])
        m["mask"] = np.full((1, 1), float(half), dtype=np.float32)
        in_maps.append(m)
    res = run_bass_kernel_spmd(nc, in_maps, core_ids=list(range(NCORES)))
    out = np.empty((Bsz * S, D), dtype=np.float32)
    for c in range(NCORES):
        bidx, half = c // 2, c % 2
        base = bidx * S
        out[base + half * T: base + (half + 1) * T] = res.results[c]["out"]
    return out.reshape(Bsz, S, D)
